# version 16
# baseline (speedup 1.0000x reference)
import numpy as np
import ml_dtypes
import concourse.bass as bass
import concourse.mybir as mybir
from concourse.bass_utils import run_bass_kernel_spmd

F32 = mybir.dt.float32
BF16 = mybir.dt.bfloat16
I32 = mybir.dt.int32
AF = mybir.ActivationFunctionType
ALU = mybir.AluOpType
AX = mybir.AxisListType

NEG_BIG = -1e30


class Cfg:
    def __init__(self, D=4096, TP=1024, NPASS=2, NW=256, NE=28, R=8, eps=1e-5):
        self.D = D
        self.TP = TP
        self.NPASS = NPASS
        self.NW = NW
        self.NE = NE
        self.R = R
        self.eps = eps
        self.KC = D // 128
        self.NT = TP // 128
        self.NSLAB = D // NW
        self.ER = NE * R
        self.ERP = 256
        self.EC = self.ERP // 128
        self.AGW = self.ER + NE
        self.NB = NW // 32
        assert D % 128 == 0 and TP % 128 == 0 and NW % 32 == 0 and D % NW == 0
        assert self.NSLAB % 4 == 0 and self.NT >= 2
        assert self.ER <= self.ERP <= 2 * 128 and self.AGW <= 256
        assert self.AGW <= self.NW
        self.SCALING = 2.0
        self.nomask = False
        self.mask_mode = "full"


def host_layouts(c: Cfg, W, mask, baseW, gateW, loraA, loraB):
    D, NW, NSLAB, KC = c.D, c.NW, c.NSLAB, c.KC

    def pm(a):
        return np.ascontiguousarray(
            a.reshape(KC, 128, NSLAB, NW).transpose(2, 1, 0, 3)
        ).reshape(NSLAB, 128, KC * NW)

    wt = pm(np.ascontiguousarray(W.T))
    mk = pm(np.ascontiguousarray(mask.T))
    bw = pm(np.ascontiguousarray(baseW.T))
    ag_f = np.concatenate([loraA.reshape(c.ER, D).T, gateW.T], axis=1)
    ag = np.ascontiguousarray(
        ag_f.reshape(KC, 128, c.AGW).transpose(1, 0, 2)).reshape(128, KC * c.AGW)
    lb_f = np.zeros((c.ERP, D), dtype=np.float32)
    lb_f[:c.ER] = loraB.transpose(0, 2, 1).reshape(c.ER, D)
    lb = np.ascontiguousarray(
        lb_f.reshape(c.EC, 128, NSLAB, NW).transpose(2, 1, 0, 3)
    ).reshape(NSLAB, 128, c.EC * NW)
    em = (np.arange(128)[:, None] // 32 == np.arange(128)[None, :] // 32)
    em = np.asarray(em.astype(ml_dtypes.bfloat16))
    idm = np.asarray(np.eye(128).astype(ml_dtypes.bfloat16))
    return dict(wt=wt.astype(np.float32), mk=mk.astype(np.int32),
                bw=bw.astype(np.float32), ag=ag.astype(np.float32),
                lb=lb.astype(np.float32), em=em, id=idm)


PH_APRE, PH_AMM, PH_APOST, PH_BH, PH_BMM, PH_BPOST = range(6)


def build(nc: bass.Bass, c: Cfg, io, lim=None):
    if lim is None:
        lim = (c.NPASS - 1, PH_BPOST)

    def on(p, ph):
        return (p, ph) <= lim

    D, TP, NP, NW = c.D, c.TP, c.NPASS, c.NW
    KC, NT, NSLAB, NB = c.KC, c.NT, c.NSLAB, c.NB
    ER, ERP, EC, AGW = c.ER, c.ERP, c.EC, c.AGW
    NE = c.NE
    rD = 1.0 / D

    x_ap, y_ap = io["x"], io["y"]
    wt_ap, mk_ap, bw_ap = io["wt"], io["mk"], io["bw"]
    ag_ap, lb_ap, em_ap = io["ag"], io["lb"], io["em"]
    id_ap = io["id"]

    wscr = nc.dram_tensor("wscr", [NSLAB, 128, KC * NW], BF16).ap()
    stg = nc.dram_tensor("stg", [NSLAB, 128, NT * NW], BF16).ap()
    x1s = nc.dram_tensor("x1s", [NT, 128, D], BF16).ap()

    def HB(p, h):
        return (2 * p + h) * NSLAB * NT

    def v_swp(p, h, w, i):
        return HB(p, h) + w * NT + i + 1

    v_ev = v_swp

    def v_pi(p, i):
        return p * NT + i + 1

    def v_wsl(p, h, w):
        return 16 * ((2 * p + h) * NSLAB + w + 1)

    def v_tpx(p, h, i):
        return 16 * (p * 2 * NT + h * NT + i + 1)

    def v_stw(p, h, w):
        return 16 * ((2 * p + h) * NSLAB + w + 1)

    def v_str(p, h, i, g):
        return 16 * (((2 * p + h) * NT + i) * 4 + g + 1)

    off = {}
    cur = 0

    def alloc_cols(name, n):
        nonlocal cur
        off[name] = cur
        cur += n

    alloc_cols("sum0", NT); alloc_cols("sq0", NT)
    alloc_cols("mu0", NT); alloc_cols("r0", NT)
    alloc_cols("acol", NT * NSLAB); alloc_cols("qcol", NT * NSLAB)
    alloc_cols("asum", NT); alloc_cols("qsum", NT)
    alloc_cols("muA", NT); alloc_cols("rA", NT)
    alloc_cols("x1sum", NT); alloc_cols("x1sq", NT)
    alloc_cols("mu1", NT); alloc_cols("r1", NT)
    alloc_cols("tvar", NT); alloc_cols("tln", NT); alloc_cols("tm2", NT)
    alloc_cols("m8", 8); alloc_cols("m8b", 8)
    alloc_cols("zap1", NE); alloc_cols("zap2", NE)
    alloc_cols("ind1", NE); alloc_cols("ind12", NE)
    alloc_cols("dcol", 1); alloc_cols("ecol", 1); alloc_cols("opl", 1)
    alloc_cols("p1", 1); alloc_cols("p2", 1); alloc_cols("t2", 1)
    alloc_cols("c1", 1); alloc_cols("c2", 1); alloc_cols("t3", NE)
    SCOLS = cur
    assert SCOLS <= 1024

    from contextlib import ExitStack
    st_ = ExitStack()
    sb = lambda name, shape, dt: st_.enter_context(nc.sbuf_tensor(name, shape, dt))
    ps = lambda name, shape: st_.enter_context(nc.psum_tensor(name, shape, F32))
    sem = lambda name: st_.enter_context(nc.semaphore(name))

    xfm = sb("xfm", [128, KC * TP], BF16)
    wsl = [sb(f"wsl{b}", [128, KC * NW], BF16) for b in range(2)]
    FLEXN = max(2 * D, 2 * NT * AGW + NT * ERP + EC * TP, KC * NW)
    flex = sb("flex", [128, FLEXN], BF16)
    STRN = max(D, NT * NW)
    strans = [sb(f"strans{b}", [128, STRN], BF16) for b in range(2)]
    restile = [sb(f"restile{b}", [128, D], F32) for b in range(2)]
    x2out = [sb(f"x2out{b}", [128, D], F32) for b in range(2)]
    stats = sb("stats", [128, SCOLS if not getattr(c, "debug_stats", False) else 1024], F32)
    pexp_all = sb("pexp_all", [128, NSLAB * KC * NB], BF16)
    bms = sb("bms", [128, 2 * KC * NB], BF16)
    lbtw = [sb(f"lbtw{b}", [128, EC * NW], BF16) for b in range(2)]
    emt = sb("emt", [128, 128], BF16)
    ident = sb("ident", [128, 128], BF16)
    dump = sb("dump", [128, NW], BF16)
    consts = sb("consts", [128, 2], F32)

    NBANK = min(NT, 7)
    pb_ = [ps(f"pb{i}", [128, 512]) for i in range(NBANK)]
    pbM = ps("pbM", [128, 512])

    def pb(i):
        return pb_[i % NBANK]

    def bank_last(i):
        b = i % NBANK
        return b + NBANK if b + NBANK < NT else b
    pbE = pbM

    def flex_x1(b):
        return flex[:, b * D:(b + 1) * D]

    flex_mask = flex[:, 0:KC * NW]
    hraw_cols = NT * AGW
    hraw = flex[:, 0:2 * hraw_cols].bitcast(F32)
    hfb_base = 2 * hraw_cols
    hfb = flex[:, hfb_base:hfb_base + NT * ERP]
    hbt_base = hfb_base + NT * ERP
    hbt = flex[:, hbt_base:hbt_base + EC * TP]
    assert hbt_base + EC * TP <= FLEXN

    def scol(name, i=0, n=1):
        o = off[name] + i
        return stats[:, o:o + n]

    pair = lambda n: [sem(n + "a"), sem(n + "b")]
    s_wsl = pair("s_wsl"); s_msl = pair("s_msl"); s_ag = sem("s_ag")
    s_lbt = pair("s_lbt"); s_x0a = pair("s_x0a"); s_x0r = pair("s_x0r")
    s_x1r = pair("s_x1r"); s_tpx = pair("s_tpx"); s_tph = pair("s_tph")
    s_stw = pair("s_stw"); s_str = pair("s_str"); s_x1w = pair("s_x1w")
    s_scw = pair("s_scw"); s_yw = pair("s_yw"); s_ini = sem("s_ini")

    def wseq(eng, pr, n):
        eng.wait_ge(pr[n % 2], 16 * (n // 2 + 1))

    def wseq_tp(eng, n):
        eng.wait_ge(s_tpx[n % 2], 32 * (n // 2 + 1))

    def iseq(ins, pr, n):
        ins.then_inc(pr[n % 2], 16)

    def wupto(eng, pr, n):
        eng.wait_ge(pr[0], 16 * (n // 2 + 1))
        if n >= 1:
            eng.wait_ge(pr[1], 16 * ((n + 1) // 2))

    def wupto_tp(eng, n):
        eng.wait_ge(s_tpx[0], 32 * (n // 2 + 1))
        if n >= 1:
            eng.wait_ge(s_tpx[1], 32 * ((n + 1) // 2))

    def e_pi(p, i):
        return p * NT + i

    def e_wsl(p, h, w):
        return (2 * p + h) * NSLAB + w

    def e_tpx(p, i):
        return p * NT + i

    def wstr_grp(eng, p, h, i):
        grp = (2 * p + h) * NT + i
        eng.wait_ge(s_str[i % 2], 64 * (grp // 2 + 1))

    def wstr_upto(eng, p, h):
        grp = (2 * p + h) * NT + NT - 1
        eng.wait_ge(s_str[1], 64 * (grp // 2 + 1))
        eng.wait_ge(s_str[0], 64 * ((grp - 1) // 2 + 1))
    c_swp = sem("c_swp"); c_ev = sem("c_ev"); c_hsw = sem("c_hsw")
    c_hev = sem("c_hev"); c_gt = sem("c_gt"); c_s0 = sem("c_s0")
    c_v0 = sem("c_v0"); c_r0 = sem("c_r0"); c_ap = sem("c_ap")
    c_vA = sem("c_vA"); c_rA = sem("c_rA"); c_pst = sem("c_pst")
    c_v1 = sem("c_v1"); c_r1 = sem("c_r1"); c_xm = sem("c_xm")
    c_vF = sem("c_vF"); c_rF = sem("c_rF"); c_y2 = sem("c_y2")
    c_msr = sem("c_msr"); c_emm = sem("c_emm"); c_mst = sem("c_mst")
    c_slb = sem("c_slb"); c_gtd = sem("c_gtd"); c_gte = sem("c_gte")
    c_vch = sem("c_vch"); c_sch = sem("c_sch"); c_s1q = sem("c_s1q")
    c_tpw = sem("c_tpw"); c_tpe = sem("c_tpe")
    assert KC % 4 == 0
    RPT = KC // 4
    _nv = {"n": 0}
    _ns = {"n": 0}

    def lhsT(k, i):
        base = (k * NT + i) * 128
        return xfm[:, base:base + 128]

    xfm4 = xfm[:].rearrange("p (k i t) -> p k i t", k=KC, i=NT, t=128)

    def strans_ck(b):
        return strans[b][:, 0:D].rearrange("t (ck k) -> t ck k", ck=KC, k=128)

    with nc.Block() as block:

        @block.gpsimd
        def _(g):
            g.dma_start(emt[:], em_ap).then_inc(s_ini, 16)
            g.dma_start(ident[:], id_ap).then_inc(s_ini, 16)
            for p in range(NP):
                KH = KC // 2 * NW
                for w in range(NSLAB if on(p, PH_AMM) else 0):
                    if p == 0 and not c.nomask and c.mask_mode not in ("r2", "r3", "r4", "r5"):
                        for hh in range(2):
                            if w >= 1:
                                g.wait_ge(c_msr, 2 * (w - 1) + hh + 1)
                            g.dma_start(flex_mask[:, hh * KH:(hh + 1) * KH],
                                        mk_ap[w][:, hh * KH:(hh + 1) * KH]
                                        ).then_inc(s_msl[hh], 16)
                    if p > 0 and w < 2:
                        continue
                    if w >= 2:
                        g.wait_ge(c_swp, v_swp(p, 0, w - 2, NT - 1))
                        if p == 0 and not c.nomask and c.mask_mode not in ("r3", "r4", "r5"):
                            wupto(g, s_scw, w - 2)
                    src = wt_ap[w] if p == 0 else wscr[w]
                    iseq(g.dma_start(wsl[w % 2][:], src), s_wsl, e_wsl(p, 0, w))
                if not on(p, PH_BH):
                    continue
                g.wait_ge(c_swp, v_swp(p, 0, NSLAB - 1, NT - 1))
                if p == 0 and not c.nomask and c.mask_mode not in ("r3", "r4", "r5"):
                    wupto(g, s_scw, NSLAB - 1)
                g.dma_start(wsl[1][:, 0:KC * AGW], ag_ap).then_inc(s_ag, 16)
                for w in range(NSLAB if on(p, PH_BMM) else 0):
                    if w == 1:
                        g.wait_ge(c_hsw, v_pi(p, NT - 1))
                    elif w >= 2:
                        g.wait_ge(c_swp, v_swp(p, 1, w - 2, NT - 1))
                    iseq(g.dma_start(wsl[w % 2][:], bw_ap[w]), s_wsl,
                         e_wsl(p, 1, w))
                    iseq(g.dma_start(lbtw[w % 2][:], lb_ap[w]), s_lbt,
                         p * NSLAB + w)
                if p + 1 < NP and on(p + 1, PH_AMM):
                    for w2 in range(2):
                        g.wait_ge(c_swp,
                                  v_swp(p, 1, NSLAB - 2 + w2, NT - 1))
                        iseq(g.dma_start(wsl[w2][:], wscr[w2]), s_wsl,
                             e_wsl(p + 1, 0, w2))
                for i in range(NT if on(p, PH_BPOST) else 0):
                    g.wait_ge(c_pst, v_pi(p, i))
                    if i >= 2:
                        g.wait_ge(c_y2, v_pi(p, i - 2))
                    wseq(g, s_x1w, e_pi(p, i))
                    iseq(g.dma_start(restile[i % 2][:], x1s[i]), s_x1r,
                         e_pi(p, i))
                    if i >= 1:
                        g.wait_ge(c_y2, v_pi(p, i - 1))
                        iseq(g.dma_start(
                            y_ap[p * TP + (i - 1) * 128:p * TP + i * 128, :],
                            x2out[(i - 1) % 2][:]), s_yw, e_pi(p, i - 1))
                if on(p, PH_BPOST):
                    g.wait_ge(c_y2, v_pi(p, NT - 1))
                    iseq(g.dma_start(
                        y_ap[p * TP + (NT - 1) * 128:p * TP + NT * 128, :],
                        x2out[(NT - 1) % 2][:]), s_yw, e_pi(p, NT - 1))

        @block.sync
        def _(sy):
            def tp_xfm(p, i, b):
                n = e_tpx(p, i)
                if n >= 2:
                    wseq_tp(sy, n - 2)
                KH2 = KC // 2
                sy.dma_start(xfm4[:, 0:KH2, i, :],
                             strans_ck(b)[:, 0:KH2, :],
                             transpose=True).then_inc(s_tpx[n % 2], 16)
                sy.dma_start(xfm4[:, KH2:KC, i, :],
                             strans_ck(b)[:, KH2:KC, :],
                             transpose=True).then_inc(s_tpx[n % 2], 16)

            def stg_reads(p, h, i):
                q = NSLAB // 4
                for g_ in range(4):
                    lo = g_ * q
                    src = stg[lo:lo + q, :, i * NW:(i + 1) * NW].transpose(
                        [1, 0, 2])
                    dst = strans[i % 2][:, lo * NW:(lo + q) * NW]
                    sy.dma_start(dst.rearrange("p (a b) -> p a b", a=q, b=NW),
                                 src).then_inc(s_str[i % 2], 16)

            for p in range(NP):
                if not on(p, PH_APRE):
                    continue
                for i in range(NT):
                    if i >= 2:
                        sy.wait_ge(c_ap, v_pi(p, i - 2))
                        sy.wait_ge(c_s0, v_pi(p, i - 2))
                    elif p > 0:
                        gp = v_pi(p - 1, NT - 2 + (i % 2))
                        sy.wait_ge(c_y2, gp)
                    iseq(sy.dma_start(
                        restile[i % 2][:],
                        x_ap[p * TP + i * 128:p * TP + (i + 1) * 128, :]),
                        s_x0a, e_pi(p, i))
                    if i >= 1:
                        sy.wait_ge(c_ap, v_pi(p, i - 1))
                        if p > 0:
                            sy.wait_ge(c_swp, HB(p, 0))
                            sy.wait_ge(c_hsw, v_pi(p - 1, NT - 1))
                        tp_xfm(p, i - 1, (i - 1) % 2)
                sy.wait_ge(c_ap, v_pi(p, NT - 1))
                if p > 0:
                    sy.wait_ge(c_swp, HB(p, 0))
                    sy.wait_ge(c_hsw, v_pi(p - 1, NT - 1))
                tp_xfm(p, NT - 1, (NT - 1) % 2)
                for w in range(NSLAB if on(p, PH_AMM) else 0):
                    if p == 0 and not c.nomask and c.mask_mode not in ("r3", "r4", "r5"):
                        sy.wait_ge(c_slb, w + 1)
                        iseq(sy.dma_start(wscr[w], wsl[w % 2][:]), s_scw, w)
                    if w == 0 and p > 0:
                        wstr_upto(sy, p - 1, 1)
                    sy.wait_ge(c_ev, v_ev(p, 0, w, NT - 1))
                    iseq(sy.dma_start(stg[w][:, 0:NT * NW],
                                      strans[w % 2][:, 0:NT * NW]),
                         s_stw, e_wsl(p, 0, w))
                for i in range(NT if on(p, PH_APOST) else 0):
                    if i == 0:
                        wupto(sy, s_stw, e_wsl(p, 0, NSLAB - 1))
                    if i >= 2:
                        sy.wait_ge(c_pst, v_pi(p, i - 2))
                        sy.wait_ge(c_xm, v_pi(p, i - 2))
                        if on(p, PH_BH):
                            sy.wait_ge(c_tpe, (p * NT + i - 1) * RPT)
                    stg_reads(p, 0, i)
                    if i < 2:
                        sy.wait_ge(c_ap, v_pi(p, NT - 2 + (i % 2)))
                        sy.wait_ge(c_s0, v_pi(p, NT - 2 + (i % 2)))
                    iseq(sy.dma_start(
                        restile[i % 2][:],
                        x_ap[p * TP + i * 128:p * TP + (i + 1) * 128, :]),
                        s_x0r, e_pi(p, i))
                    sy.wait_ge(c_pst, v_pi(p, i))
                    iseq(sy.dma_start(x1s[i], strans[i % 2][:, 0:D]), s_x1w,
                         e_pi(p, i))
                for i in range(NT if on(p, PH_BH) else 0):
                    sy.wait_ge(c_gt, v_pi(p, i))
                    if e_pi(p, i) >= 2:
                        wseq(sy, s_tph, e_pi(p, i) - 2)
                    iseq(sy.dma_start(
                        hbt.rearrange("p (ck z) -> p ck z", ck=EC, z=TP)[
                            :, :, i * 128:(i + 1) * 128],
                        hfb[:, i * ERP:(i + 1) * ERP].rearrange(
                            "t (ck z) -> t ck z", ck=EC, z=128),
                        transpose=True), s_tph, e_pi(p, i))
                for w in range(NSLAB if on(p, PH_BMM) else 0):
                    if w == 0:
                        wstr_upto(sy, p, 0)
                    sy.wait_ge(c_ev, v_ev(p, 1, w, NT - 1))
                    iseq(sy.dma_start(stg[w][:, 0:NT * NW],
                                      strans[w % 2][:, 0:NT * NW]),
                         s_stw, e_wsl(p, 1, w))
                for i in range(NT if on(p, PH_BPOST) else 0):
                    if i == 0:
                        wupto(sy, s_stw, e_wsl(p, 1, NSLAB - 1))
                    if i >= 2:
                        sy.wait_ge(c_y2, v_pi(p, i - 2))
                    stg_reads(p, 1, i)
            lp, lph = lim
            if lph >= PH_BPOST:
                wupto(sy, s_yw, lp * NT + NT - 1)
                if "dbg" in io:
                    sy.dma_start(io["dbg"], stats[:]).then_inc(s_ini, 16)
                    sy.wait_ge(s_ini, 48)
            elif lph >= PH_BMM:
                wupto(sy, s_stw, e_wsl(lp, 1, NSLAB - 1))
            elif lph >= PH_BH:
                wupto(sy, s_tph, lp * NT + NT - 1)
            elif lph >= PH_APOST:
                wupto(sy, s_x1w, lp * NT + NT - 1)
            elif lph >= PH_AMM:
                wupto(sy, s_stw, e_wsl(lp, 0, NSLAB - 1))
            else:
                wupto_tp(sy, e_tpx(lp, NT - 1))

        @block.tensor
        def _(t):
            def emm(we):
                B0 = (we % 2) * KC * NB
                for ck in range(KC):
                    ins_ = t.matmul(
                        pbE[:, 256 + ck * NB:256 + (ck + 1) * NB],
                        emt[:], bms[:, B0 + ck * NB:B0 + (ck + 1) * NB],
                        start=True, stop=True)
                return ins_

            _tp0 = {"first": True}
            for p in range(NP):
                if not on(p, PH_AMM):
                    continue
                if p == 0 and not c.nomask and c.mask_mode not in ("red", "r1", "r2", "r3", "r4", "r5"):
                    t.wait_ge(s_ini, 32)
                    t.wait_ge(c_msr, 2)
                    emm(0).then_inc(c_emm, 1)
                for w in range(NSLAB):
                    for i in range(NT):
                        if w == 0:
                            wseq_tp(t, e_tpx(p, i))
                        if p == 0 and not c.nomask and c.mask_mode != "r5":
                            t.wait_ge(c_slb, w + 1)
                        else:
                            wseq(t, s_wsl, e_wsl(p, 0, w))
                        if i >= NBANK:
                            t.wait_ge(c_ev, v_ev(p, 0, w, i - NBANK))
                        elif w >= 1:
                            t.wait_ge(c_ev, v_ev(p, 0, w - 1, bank_last(i)))
                        elif HB(p, 0) > 0:
                            t.wait_ge(c_ev, HB(p, 0))
                        for k in range(KC):
                            ins_ = t.matmul(pb(i)[:, 0:NW], lhsT(k, i),
                                            wsl[w % 2][:, k * NW:(k + 1) * NW],
                                            start=(k == 0), stop=(k == KC - 1))
                        ins_.then_inc(c_swp, 1)
                        if (p == 0 and not c.nomask and i == 0
                                and w + 1 < NSLAB and c.mask_mode not in ("red", "r1", "r2", "r3", "r4", "r5")):
                            we = w + 1
                            t.wait_ge(c_msr, 2 * (we + 1))
                            t.wait_ge(c_mst, we)
                            emm(we).then_inc(c_emm, 1)
                def h_sweep(i):
                    t.wait_ge(s_ag, 16 * (p + 1))
                    t.wait_ge(c_tpe, (p * NT + i + 1) * RPT)
                    if i >= NBANK:
                        t.wait_ge(c_ev, v_ev(p, 1, 0, i - NBANK))
                    else:
                        t.wait_ge(c_ev, v_ev(p, 0, NSLAB - 1, bank_last(i)))
                    for k in range(KC):
                        ins_ = t.matmul(pb(i)[:, 256:256 + AGW], lhsT(k, i),
                                        wsl[1][:, k * AGW:(k + 1) * AGW],
                                        start=(k == 0), stop=(k == KC - 1))
                    ins_.then_inc(c_hsw, 1)

                def base0(i):
                    wseq(t, s_wsl, e_wsl(p, 1, 0))
                    t.wait_ge(c_hev, v_pi(p, i))
                    for k in range(KC):
                        t.matmul(pb(i)[:, 0:NW], lhsT(k, i),
                                 wsl[0][:, k * NW:(k + 1) * NW],
                                 start=(k == 0), stop=False)

                def lora0(i):
                    wseq(t, s_lbt, p * NSLAB + 0)
                    wseq(t, s_tph, e_pi(p, i))
                    for ck in range(EC):
                        ins_ = t.matmul(
                            pb(i)[:, 0:NW],
                            hbt[:, ck * TP + i * 128:ck * TP + (i + 1) * 128],
                            lbtw[0][:, ck * NW:(ck + 1) * NW],
                            start=False, stop=(ck == EC - 1))
                    ins_.then_inc(c_swp, 1)

                def tp_pe(i):
                    gbase = (p * NT + i) * RPT
                    t.wait_ge(c_xm, v_pi(p, i))
                    if _tp0["first"]:
                        t.wait_ge(s_ini, 32)
                        _tp0["first"] = False
                    if i >= NBANK and on(p, PH_BMM):
                        t.wait_ge(c_ev, v_ev(p, 1, 0, i - NBANK))
                    else:
                        t.wait_ge(c_ev, v_ev(p, 0, NSLAB - 1, bank_last(i)))
                    for r in range(RPT):
                        g_ = gbase + r
                        if g_ >= 2:
                            t.wait_ge(c_tpe, g_ - 1)
                        bank = pbM if r % 2 == 0 else pb(i)
                        for cc in range(4):
                            k = r * 4 + cc
                            ins_ = t.matmul(
                                bank[:, cc * 128:(cc + 1) * 128],
                                strans[i % 2][:, k * 128:(k + 1) * 128],
                                ident[:], start=True, stop=True)
                        ins_.then_inc(c_tpw, 1)

                if on(p, PH_BH):
                    for i in range(NT):
                        if i >= NBANK and on(p, PH_BMM):
                            lora0(i - NBANK)
                        tp_pe(i)
                        h_sweep(i)
                        if i >= 1 and on(p, PH_BMM):
                            base0(i - 1)
                    if on(p, PH_BMM):
                        base0(NT - 1)
                        for j in range(max(NT - NBANK, 0), NT):
                            lora0(j)
                for w in (range(1, NSLAB) if on(p, PH_BMM) else []):
                    for i in range(NT):
                        wseq(t, s_wsl, e_wsl(p, 1, w))
                        wseq(t, s_lbt, p * NSLAB + w)
                        if i >= NBANK:
                            t.wait_ge(c_ev, v_ev(p, 1, w, i - NBANK))
                        else:
                            t.wait_ge(c_ev, v_ev(p, 1, w - 1, bank_last(i)))
                        for k in range(KC):
                            t.matmul(pb(i)[:, 0:NW], lhsT(k, i),
                                     wsl[w % 2][:, k * NW:(k + 1) * NW],
                                     start=(k == 0), stop=False)
                        for ck in range(EC):
                            ins_ = t.matmul(
                                pb(i)[:, 0:NW],
                                hbt[:, ck * TP + i * 128:ck * TP + (i + 1) * 128],
                                lbtw[w % 2][:, ck * NW:(ck + 1) * NW],
                                start=False, stop=(ck == EC - 1))
                        ins_.then_inc(c_swp, 1)

        @block.scalar
        def _(sc):
            _ne = {"n": 0}

            def rstd_chain(i, dst, done_sem):
                sc.activation(scol("tln", i), scol("tvar", i), AF.Ln,
                              bias=consts[:, 0:1]).then_inc(c_sch, 1)
                _ns["n"] += 1
                sc.wait_ge(c_sch, _ns["n"])
                sc.activation(scol(dst, i), scol("tln", i), AF.Exp,
                              scale=consts[:, 1:2]).then_inc(done_sem, 1)

            for p in range(NP):
                for i in range(NT if on(p, PH_APRE) else 0):
                    wseq(sc, s_x0a, e_pi(p, i))
                    if i >= 2:
                        wseq_tp(sc, e_tpx(p, i - 2))
                    sc.activation(strans[i % 2][:, 0:D], restile[i % 2][:], AF.Square,
                                  accum_out=scol("sq0", i)).then_inc(c_s0, 1)
                    sc.wait_ge(c_v0, v_pi(p, i))
                    rstd_chain(i, "r0", c_r0)
                for w in range(NSLAB if on(p, PH_AMM) else 0):
                    for i in range(NT):
                        sc.wait_ge(c_swp, v_swp(p, 0, w, i))
                        if w == 0 and i == 0:
                            wupto_tp(sc, e_tpx(p, NT - 1))
                        if w >= 2 and i == 0:
                            wseq(sc, s_stw, e_wsl(p, 0, w - 2))
                        sc.activation(strans[w % 2][:, i * NW:(i + 1) * NW],
                                      pb(i)[:, 0:NW], AF.Copy,
                                      accum_out=scol("acol", i * NSLAB + w))
                        sc.wait_ge(c_ev, _ne["n"])
                        sc.activation(dump[:], pb(i)[:, 0:NW], AF.Square,
                                      accum_out=scol("qcol", i * NSLAB + w)
                                      ).then_inc(c_ev, 1)
                        _ne["n"] += 1
                def evacB(w, i):
                    sc.wait_ge(c_swp, v_swp(p, 1, w, i))
                    if w == 0 and i == 0:
                        sc.wait_ge(c_tpe, (p * NT + NT - 1) * RPT)
                    if w == 1 and i == 0:
                        sc.wait_ge(c_tpe, (p * NT + NT) * RPT)
                    if w >= 2 and i == 0:
                        wseq(sc, s_stw, e_wsl(p, 1, w - 2))
                    sc.activation(strans[w % 2][:, i * NW:(i + 1) * NW],
                                  pb(i)[:, 0:NW], AF.Copy,
                                  accum_out=scol("acol", i * NSLAB + w))
                    sc.wait_ge(c_ev, _ne["n"])
                    sc.activation(dump[:], pb(i)[:, 0:NW], AF.Square,
                                  accum_out=scol("qcol", i * NSLAB + w)
                                  ).then_inc(c_ev, 1)
                    _ne["n"] += 1

                def tp_evac(i):
                    gbase = (p * NT + i) * RPT
                    for r in range(RPT):
                        sc.wait_ge(c_tpw, gbase + r + 1)
                        bank = pbM if r % 2 == 0 else pb(i)
                        sc.activation(xfm4[:, 4 * r:4 * r + 4, i, :],
                                      bank[:].rearrange("p (a b) -> p a b",
                                                        a=4, b=128),
                                      AF.Copy).then_inc(c_tpe, 1)

                def hev_gate(j):
                    sc.wait_ge(c_hsw, v_pi(p, j))
                    sc.activation(hraw[:, j * AGW:(j + 1) * AGW],
                                  pb(j)[:, 256:256 + AGW], AF.Copy
                                  ).then_inc(c_hev, 1)
                    sc.wait_ge(c_gtd, v_pi(p, j))
                    sc.activation(scol("ecol"), scol("dcol"), AF.Exp
                                  ).then_inc(c_gte, 1)

                for i in range(NT if on(p, PH_APOST) else 0):
                    sc.wait_ge(c_vA, v_pi(p, i))
                    rstd_chain(i, "rA", c_rA)
                    sc.wait_ge(c_pst, v_pi(p, i))
                    sc.activation(x2out[i % 2][:], strans[i % 2][:, 0:D],
                                  AF.Square,
                                  accum_out=scol("x1sq", i)).then_inc(c_s1q, 1)
                    if i >= 1 and on(p, PH_BH):
                        if i - 1 >= NBANK and on(p, PH_BMM):
                            evacB(0, i - 1 - NBANK)
                        hev_gate(i - 1)
                    sc.wait_ge(c_v1, v_pi(p, i))
                    rstd_chain(i, "r1", c_r1)
                    if on(p, PH_BH) and (i < NT - 1 or NT - 1 < NBANK):
                        tp_evac(i)
                if on(p, PH_BH):
                    if NT - 1 >= NBANK:
                        if on(p, PH_BMM):
                            evacB(0, NT - 1 - NBANK)
                        tp_evac(NT - 1)
                    hev_gate(NT - 1)
                if on(p, PH_BH) and on(p, PH_BMM):
                    for j in range(max(NT - NBANK, 0), NT):
                        evacB(0, j)
                for w in (range(1, NSLAB) if on(p, PH_BMM) else []):
                    for i in range(NT):
                        evacB(w, i)
                for i in range(NT if on(p, PH_BPOST) else 0):
                    sc.wait_ge(c_vF, v_pi(p, i))
                    rstd_chain(i, "rA", c_rF)

        @block.vector
        def _(v):
            def vf(ins):
                ins.then_inc(c_vch, 1)
                _nv["n"] += 1

            def vw():
                v.wait_ge(c_vch, _nv["n"])

            def var_chain(sumc, sqc, muc, i):
                vw()
                vf(v.tensor_scalar(scol(muc, i), scol(sumc, i), rD, None,
                                   ALU.mult))
                vw()
                vf(v.tensor_tensor(scol("tm2", i), scol(muc, i), scol(muc, i),
                                   ALU.mult))
                vw()
                return v.scalar_tensor_tensor(scol("tvar", i), scol(sqc, i), rD,
                                              scol("tm2", i), ALU.mult,
                                              ALU.subtract)

            v.memset(consts[:, 0:1], float(c.eps))
            v.memset(consts[:, 1:2], -0.5)
            for p in range(NP):
                state = {"mask_w": 0, "prep_w": 0}

                def emit_mask_prep():
                    w = state["prep_w"]
                    B0 = (w % 2) * KC * NB
                    if c.mask_mode in ("r1", "r2", "r3", "r4", "r5"):
                        v.memset(bms[:, B0:B0 + KC * NB], 0.0).then_inc(c_msr, 1)
                        v.memset(bms[:, B0:B0 + KC * NB], 0.0).then_inc(c_msr, 1)
                    else:
                        KH2 = KC // 2 * NW
                        BH2 = KC // 2 * NB
                        if w >= 2:
                            v.wait_ge(c_emm, w - 1)
                        for hh in range(2):
                            v.wait_ge(s_msl[hh], 16 * (w + 1))
                            with nc.allow_low_precision(
                                    reason="mask 0/1 block sums <=32 exact"):
                                v.tensor_reduce(
                                    bms[:, B0 + hh * BH2:B0 + (hh + 1) * BH2],
                                    flex_mask[:, hh * KH2:(hh + 1) * KH2
                                              ].rearrange(
                                        "p (ck b z) -> p ck b z", ck=KC // 2,
                                        b=NB, z=32),
                                    AX.X, ALU.add).then_inc(c_msr, 1)
                    state["prep_w"] += 1

                def emit_mask_mult():
                    w = state["mask_w"]
                    if c.mask_mode not in ("red", "r1", "r2", "r3", "r4", "r5"):
                        v.wait_ge(c_emm, w + 1)
                        v.tensor_scalar(pexp_all[:, w * KC * NB:(w + 1) * KC * NB],
                                        pbE[:, 256:256 + KC * NB], 1.0,
                                        None, ALU.min).then_inc(c_mst, 1)
                    wseq(v, s_wsl, e_wsl(p, 0, w))
                    if c.mask_mode == "full":
                        v.wait_ge(c_mst, w + 1)
                        v4 = wsl[w % 2][:].rearrange("p (ck b z) -> p ck b z",
                                                     ck=KC, b=NB, z=32)
                        v.tensor_tensor(
                            v4, v4,
                            pexp_all[:, w * KC * NB:(w + 1) * KC * NB
                                     ].rearrange("p (ck b) -> p ck b", ck=KC,
                                                 b=NB).unsqueeze(3
                                                 ).to_broadcast(
                                                     [128, KC, NB, 32]),
                            ALU.mult).then_inc(c_slb, 1)
                    elif c.mask_mode in ("r4", "r5"):
                        v.memset(bms[:], 1.0).then_inc(c_slb, 1)
                    else:
                        v.tensor_scalar(wsl[w % 2][:], wsl[w % 2][:], 1.0,
                                        None, ALU.mult).then_inc(c_slb, 1)
                    state["mask_w"] += 1

                for i in range(NT if on(p, PH_APRE) else 0):
                    wseq(v, s_x0a, e_pi(p, i))
                    vf(v.tensor_reduce(scol("sum0", i), restile[i % 2][:], AX.X,
                                       ALU.add))
                    v.wait_ge(c_s0, v_pi(p, i))
                    var_chain("sum0", "sq0", "mu0", i).then_inc(c_v0, 1)
                    v.wait_ge(c_r0, v_pi(p, i))
                    v.tensor_scalar(strans[i % 2][:, 0:D], restile[i % 2][:],
                                    scol("mu0", i), scol("r0", i),
                                    ALU.subtract, ALU.mult).then_inc(c_ap, 1)
                    if p == 0 and on(p, PH_AMM) and not c.nomask:
                        while state["prep_w"] < min(i + 2, 3, NSLAB):
                            emit_mask_prep()
                        while state["mask_w"] < min(i + 1, 2):
                            emit_mask_mult()
                if p == 0 and on(p, PH_AMM) and not c.nomask:
                    while state["mask_w"] < NSLAB or state["prep_w"] < NSLAB:
                        if state["prep_w"] < min(state["mask_w"] + 1, NSLAB):
                            emit_mask_prep()
                        else:
                            emit_mask_mult()
                def _gating_body(i):
                    lg = hraw[:, i * AGW + ER:(i + 1) * AGW]
                    vw()
                    vf(v.max(out=stats[:, off["m8"]:off["m8"] + 8], in_=lg))
                    vw()
                    vf(v.memset(stats[:, off["m8"] + 2:off["m8"] + 8], NEG_BIG))
                    vw()
                    vf(v.match_replace(out=scol("zap2", 0, NE),
                                       in_to_replace=stats[:, off["m8"]:off["m8"] + 8],
                                       in_values=lg, imm_value=NEG_BIG))
                    vw()
                    vf(v.tensor_copy(stats[:, off["m8b"]:off["m8b"] + 8],
                                     stats[:, off["m8"]:off["m8"] + 8]))
                    vw()
                    vf(v.memset(stats[:, off["m8b"] + 1:off["m8b"] + 8],
                                NEG_BIG))
                    vw()
                    vf(v.match_replace(out=scol("zap1", 0, NE),
                                       in_to_replace=stats[:, off["m8b"]:off["m8b"] + 8],
                                       in_values=lg, imm_value=NEG_BIG))
                    vw()
                    vf(v.tensor_tensor(scol("ind1", 0, NE), scol("zap1", 0, NE),
                                       lg, ALU.not_equal))
                    vf(v.tensor_tensor(scol("ind12", 0, NE),
                                       scol("zap2", 0, NE), lg, ALU.not_equal))
                    v.tensor_tensor(scol("dcol"),
                                    stats[:, off["m8"] + 1:off["m8"] + 2],
                                    stats[:, off["m8"]:off["m8"] + 1],
                                    ALU.subtract).then_inc(c_gtd, 1)
                    v.wait_ge(c_gte, v_pi(p, i))
                    vf(v.tensor_scalar(scol("opl"), scol("ecol"), 1.0, None,
                                       ALU.add))
                    vw()
                    vf(v.reciprocal(scol("p1"), scol("opl")))
                    vw()
                    vf(v.tensor_tensor(scol("p2"), scol("ecol"), scol("p1"),
                                       ALU.mult))
                    vw()
                    vf(v.tensor_tensor(scol("t2"), scol("p1"), scol("p2"),
                                       ALU.subtract))
                    vw()
                    vf(v.tensor_scalar(scol("c1"), scol("t2"), c.SCALING, None,
                                       ALU.mult))
                    vf(v.tensor_scalar(scol("c2"), scol("p2"), c.SCALING, None,
                                       ALU.mult))
                    vw()
                    vf(v.tensor_scalar(scol("t3", 0, NE), scol("ind12", 0, NE),
                                       scol("c2"), None, ALU.mult))
                    vw()
                    vf(v.scalar_tensor_tensor(scol("ind1", 0, NE),
                                              scol("ind1", 0, NE), scol("c1"),
                                              scol("t3", 0, NE), ALU.mult,
                                              ALU.add))
                    vw()
                    v.tensor_tensor(
                        hfb[:, i * ERP:i * ERP + ER].rearrange(
                            "p (e r) -> p e r", e=NE, r=c.R),
                        hraw[:, i * AGW:i * AGW + ER].rearrange(
                            "p (e r) -> p e r", e=NE, r=c.R),
                        scol("ind1", 0, NE).unsqueeze(2).to_broadcast(
                            [128, NE, c.R]),
                        ALU.mult).then_inc(c_gt, 1)

                def emit_gating(i):
                    if not on(p, PH_BH):
                        return
                    v.wait_ge(c_hev, v_pi(p, i))
                    _gating_body(i)

                if on(p, PH_BH) and on(p, PH_APOST):
                    vf(v.memset(hfb[:], 0.0))
                for i in range(NT if on(p, PH_APOST) else 0):
                    v.wait_ge(c_ev, v_ev(p, 0, NSLAB - 1, NT - 1))
                    v.tensor_reduce(scol("asum", i),
                                    scol("acol", i * NSLAB, NSLAB), AX.X,
                                    ALU.add)
                    vf(v.tensor_reduce(scol("qsum", i),
                                       scol("qcol", i * NSLAB, NSLAB), AX.X,
                                       ALU.add))
                    var_chain("asum", "qsum", "muA", i).then_inc(c_vA, 1)
                    v.wait_ge(c_rA, v_pi(p, i))
                    wstr_grp(v, p, 0, i)
                    wseq(v, s_x0r, e_pi(p, i))
                    if p > 0 and i < 2:
                        wseq(v, s_yw, e_pi(p - 1, NT - 2 + (i % 2)))
                    if i >= 2:
                        v.wait_ge(c_s1q, v_pi(p, i - 2))
                    vf(v.tensor_scalar(x2out[i % 2][:], strans[i % 2][:, 0:D],
                                       scol("muA", i), scol("rA", i),
                                       ALU.subtract, ALU.mult))
                    vw()
                    v.scalar_tensor_tensor(strans[i % 2][:, 0:D],
                                           x2out[i % 2][:],
                                           1.0, restile[i % 2][:], ALU.mult,
                                           ALU.add,
                                           accum_out=scol("x1sum", i)
                                           ).then_inc(c_pst, 1)
                    v.wait_ge(c_pst, v_pi(p, i))
                    if i >= 1:
                        emit_gating(i - 1)
                    v.wait_ge(c_s1q, v_pi(p, i))
                    var_chain("x1sum", "x1sq", "mu1", i).then_inc(c_v1, 1)
                    v.wait_ge(c_r1, v_pi(p, i))
                    wseq(v, s_x1w, e_pi(p, i))
                    v.tensor_scalar(strans[i % 2][:, 0:D],
                                    strans[i % 2][:, 0:D],
                                    scol("mu1", i), scol("r1", i),
                                    ALU.subtract, ALU.mult).then_inc(c_xm, 1)
                if on(p, PH_BH):
                    emit_gating(NT - 1)
                for i in range(NT if on(p, PH_BPOST) else 0):
                    v.wait_ge(c_ev, v_ev(p, 1, NSLAB - 1, NT - 1))
                    v.tensor_reduce(scol("asum", i),
                                    scol("acol", i * NSLAB, NSLAB), AX.X,
                                    ALU.add)
                    vf(v.tensor_reduce(scol("qsum", i),
                                       scol("qcol", i * NSLAB, NSLAB), AX.X,
                                       ALU.add))
                    var_chain("asum", "qsum", "muA", i).then_inc(c_vF, 1)
                    v.wait_ge(c_rF, v_pi(p, i))
                    wstr_grp(v, p, 1, i)
                    wseq(v, s_x1r, e_pi(p, i))
                    if i >= 2:
                        wseq(v, s_yw, e_pi(p, i - 2))
                    vf(v.tensor_scalar(x2out[i % 2][:], strans[i % 2][:, 0:D],
                                       scol("muA", i), scol("rA", i),
                                       ALU.subtract, ALU.mult))
                    vw()
                    v.scalar_tensor_tensor(x2out[i % 2][:], x2out[i % 2][:], 1.0,
                                           restile[i % 2][:], ALU.mult, ALU.add
                                           ).then_inc(c_y2, 1)

    st_.close()
    return nc


def _run(inputs, c: Cfg, n_cores=8, lim=None):
    T = c.NPASS * c.TP
    x = np.ascontiguousarray(
        np.asarray(inputs["L0_attn_data"], dtype=np.float32).reshape(-1, c.D))
    host = host_layouts(
        c,
        np.asarray(inputs["L0_attn_weight"], dtype=np.float32),
        np.asarray(inputs["L0_attn_mask"], dtype=np.int32),
        np.asarray(inputs["L0_ffn_base_W"], dtype=np.float32),
        np.asarray(inputs["L0_ffn_gate_W"], dtype=np.float32),
        np.asarray(inputs["L0_ffn_lora_A"], dtype=np.float32),
        np.asarray(inputs["L0_ffn_lora_B"], dtype=np.float32),
    )
    nc = bass.Bass("TRN2", target_bir_lowering=False)
    io = {}
    io["x"] = nc.dram_tensor("x", [T, c.D], F32, kind="ExternalInput").ap()
    io["y"] = nc.dram_tensor("y", [T, c.D], F32, kind="ExternalOutput").ap()
    for nm, dt_ in [("wt", F32), ("mk", I32), ("bw", F32), ("ag", F32),
                    ("lb", F32)]:
        io[nm] = nc.dram_tensor(nm, list(host[nm].shape), dt_,
                                kind="ExternalInput").ap()
    io["em"] = nc.dram_tensor("em", [128, 128], BF16, kind="ExternalInput").ap()
    io["id"] = nc.dram_tensor("id", [128, 128], BF16, kind="ExternalInput").ap()
    if getattr(c, "debug_stats", False):
        io["dbg"] = nc.dram_tensor("dbg", [128, 1024], F32,
                                   kind="ExternalOutput").ap()
    build(nc, c, io, lim=lim)
    in_maps = []
    for ci in range(n_cores):
        m = dict(host)
        m["x"] = np.ascontiguousarray(x[ci * T:(ci + 1) * T])
        in_maps.append(m)
    res = run_bass_kernel_spmd(nc, in_maps, core_ids=list(range(n_cores)))
    out = np.concatenate([r["y"] for r in res.results], axis=0)
    if getattr(c, "debug_stats", False):
        return out.reshape(1, -1, c.D), [r["dbg"] for r in res.results]
    return out.reshape(1, -1, c.D)


def kernel(L0_attn_data, L0_attn_weight, L0_attn_mask, L0_ffn_base_W,
           L0_ffn_gate_W, L0_ffn_lora_A, L0_ffn_lora_B):
    c = Cfg()
    return _run(dict(
        L0_attn_data=L0_attn_data, L0_attn_weight=L0_attn_weight,
        L0_attn_mask=L0_attn_mask, L0_ffn_base_W=L0_ffn_base_W,
        L0_ffn_gate_W=L0_ffn_gate_W, L0_ffn_lora_A=L0_ffn_lora_A,
        L0_ffn_lora_B=L0_ffn_lora_B), c)


# version 17 (final)
# speedup vs baseline: 1.0327x; 1.0327x over previous
import numpy as np
import ml_dtypes
import concourse.bass as bass
import concourse.mybir as mybir
from concourse.bass_utils import run_bass_kernel_spmd

F32 = mybir.dt.float32
BF16 = mybir.dt.bfloat16
I32 = mybir.dt.int32
AF = mybir.ActivationFunctionType
ALU = mybir.AluOpType
AX = mybir.AxisListType

NEG_BIG = -1e30


class Cfg:
    def __init__(self, D=4096, TP=1024, NPASS=2, NW=256, NE=28, R=8, eps=1e-5):
        self.D = D
        self.TP = TP
        self.NPASS = NPASS
        self.NW = NW
        self.NE = NE
        self.R = R
        self.eps = eps
        self.KC = D // 128
        self.NT = TP // 128
        self.NSLAB = D // NW
        self.ER = NE * R
        self.ERP = 256
        self.EC = self.ERP // 128
        self.AGW = self.ER + NE
        self.NB = NW // 32
        assert D % 128 == 0 and TP % 128 == 0 and NW % 32 == 0 and D % NW == 0
        assert self.NSLAB % 4 == 0 and self.NT >= 2
        assert self.ER <= self.ERP <= 2 * 128 and self.AGW <= 256
        assert self.AGW <= self.NW
        self.SCALING = 2.0
        self.nomask = False
        self.mask_mode = "full"


def host_layouts(c: Cfg, W, mask, baseW, gateW, loraA, loraB):
    D, NW, NSLAB, KC = c.D, c.NW, c.NSLAB, c.KC

    def pm(a):
        return np.ascontiguousarray(
            a.reshape(KC, 128, NSLAB, NW).transpose(2, 1, 0, 3)
        ).reshape(NSLAB, 128, KC * NW)

    wt = pm(np.ascontiguousarray(W.T))
    mk = pm(np.ascontiguousarray(mask.T))
    bw = pm(np.ascontiguousarray(baseW.T))
    ag_f = np.concatenate([loraA.reshape(c.ER, D).T, gateW.T], axis=1)
    ag = np.ascontiguousarray(
        ag_f.reshape(KC, 128, c.AGW).transpose(1, 0, 2)).reshape(128, KC * c.AGW)
    lb_f = np.zeros((c.ERP, D), dtype=np.float32)
    lb_f[:c.ER] = loraB.transpose(0, 2, 1).reshape(c.ER, D)
    lb = np.ascontiguousarray(
        lb_f.reshape(c.EC, 128, NSLAB, NW).transpose(2, 1, 0, 3)
    ).reshape(NSLAB, 128, c.EC * NW)
    em = (np.arange(128)[:, None] // 32 == np.arange(128)[None, :] // 32)
    em = np.asarray(em.astype(ml_dtypes.bfloat16))
    idm = np.asarray(np.eye(128).astype(ml_dtypes.bfloat16))
    return dict(wt=wt.astype(np.float32), mk=mk.astype(np.int32),
                bw=bw.astype(np.float32), ag=ag.astype(np.float32),
                lb=lb.astype(np.float32), em=em, id=idm)


PH_APRE, PH_AMM, PH_APOST, PH_BH, PH_BMM, PH_BPOST = range(6)


def build(nc: bass.Bass, c: Cfg, io, lim=None):
    if lim is None:
        lim = (c.NPASS - 1, PH_BPOST)

    def on(p, ph):
        return (p, ph) <= lim

    D, TP, NP, NW = c.D, c.TP, c.NPASS, c.NW
    KC, NT, NSLAB, NB = c.KC, c.NT, c.NSLAB, c.NB
    ER, ERP, EC, AGW = c.ER, c.ERP, c.EC, c.AGW
    NE = c.NE
    rD = 1.0 / D

    x_ap, y_ap = io["x"], io["y"]
    wt_ap, mk_ap, bw_ap = io["wt"], io["mk"], io["bw"]
    ag_ap, lb_ap, em_ap = io["ag"], io["lb"], io["em"]
    id_ap = io["id"]

    wscr = nc.dram_tensor("wscr", [NSLAB, 128, KC * NW], BF16).ap()
    stg = nc.dram_tensor("stg", [NSLAB, 128, NT * NW], BF16).ap()
    x1s = nc.dram_tensor("x1s", [NT, 128, D], BF16).ap()

    def HB(p, h):
        return (2 * p + h) * NSLAB * NT

    def v_swp(p, h, w, i):
        return HB(p, h) + w * NT + i + 1

    v_ev = v_swp

    def v_pi(p, i):
        return p * NT + i + 1

    def v_wsl(p, h, w):
        return 16 * ((2 * p + h) * NSLAB + w + 1)

    def v_tpx(p, h, i):
        return 16 * (p * 2 * NT + h * NT + i + 1)

    def v_stw(p, h, w):
        return 16 * ((2 * p + h) * NSLAB + w + 1)

    def v_str(p, h, i, g):
        return 16 * (((2 * p + h) * NT + i) * 4 + g + 1)

    off = {}
    cur = 0

    def alloc_cols(name, n):
        nonlocal cur
        off[name] = cur
        cur += n

    alloc_cols("sum0", NT); alloc_cols("sq0", NT)
    alloc_cols("mu0", NT); alloc_cols("r0", NT)
    alloc_cols("acol", NT * NSLAB); alloc_cols("qcol", NT * NSLAB)
    alloc_cols("asum", NT); alloc_cols("qsum", NT)
    alloc_cols("muA", NT); alloc_cols("rA", NT)
    alloc_cols("x1sum", NT); alloc_cols("x1sq", NT)
    alloc_cols("mu1", NT); alloc_cols("r1", NT)
    alloc_cols("tvar", NT); alloc_cols("tln", NT); alloc_cols("tm2", NT)
    alloc_cols("m8", 8); alloc_cols("m8b", 8)
    alloc_cols("zap1", NE); alloc_cols("zap2", NE)
    alloc_cols("ind1", NE); alloc_cols("ind12", NE)
    alloc_cols("dcol", 1); alloc_cols("ecol", 1); alloc_cols("opl", 1)
    alloc_cols("p1", 1); alloc_cols("p2", 1); alloc_cols("t2", 1)
    alloc_cols("c1", 1); alloc_cols("c2", 1); alloc_cols("t3", NE)
    SCOLS = cur
    assert SCOLS <= 1024

    from contextlib import ExitStack
    st_ = ExitStack()
    sb = lambda name, shape, dt: st_.enter_context(nc.sbuf_tensor(name, shape, dt))
    ps = lambda name, shape: st_.enter_context(nc.psum_tensor(name, shape, F32))
    sem = lambda name: st_.enter_context(nc.semaphore(name))

    xfm = sb("xfm", [128, KC * TP], BF16)
    wsl = [sb(f"wsl{b}", [128, KC * NW], BF16) for b in range(2)]
    FLEXN = max(2 * D, 2 * NT * AGW + NT * ERP + EC * TP, KC * NW)
    flex = sb("flex", [128, FLEXN], BF16)
    STRN = max(D, NT * NW)
    strans = [sb(f"strans{b}", [128, STRN], BF16) for b in range(2)]
    restile = [sb(f"restile{b}", [128, D], F32) for b in range(2)]
    x2out = [sb(f"x2out{b}", [128, D], F32) for b in range(2)]
    stats = sb("stats", [128, SCOLS if not getattr(c, "debug_stats", False) else 1024], F32)
    pexp_all = sb("pexp_all", [128, NSLAB * KC * NB], BF16)
    bms = sb("bms", [128, 2 * KC * NB], BF16)
    lbtw = [sb(f"lbtw{b}", [128, EC * NW], BF16) for b in range(2)]
    emt = sb("emt", [128, 128], BF16)
    ident = sb("ident", [128, 128], BF16)
    dump = sb("dump", [128, NW], BF16)
    consts = sb("consts", [128, 2], F32)

    NBANK = min(NT, 7)
    pb_ = [ps(f"pb{i}", [128, 512]) for i in range(NBANK)]
    pbM = ps("pbM", [128, 512])

    def pb(i):
        return pb_[i % NBANK]

    def bank_last(i):
        b = i % NBANK
        return b + NBANK if b + NBANK < NT else b
    pbE = pbM

    def flex_x1(b):
        return flex[:, b * D:(b + 1) * D]

    flex_mask = flex[:, 0:KC * NW]
    hraw_cols = NT * AGW
    hraw = flex[:, 0:2 * hraw_cols].bitcast(F32)
    hfb_base = 2 * hraw_cols
    hfb = flex[:, hfb_base:hfb_base + NT * ERP]
    hbt_base = hfb_base + NT * ERP
    hbt = flex[:, hbt_base:hbt_base + EC * TP]
    assert hbt_base + EC * TP <= FLEXN

    def scol(name, i=0, n=1):
        o = off[name] + i
        return stats[:, o:o + n]

    pair = lambda n: [sem(n + "a"), sem(n + "b")]
    s_wsl = pair("s_wsl"); s_msl = pair("s_msl"); s_ag = sem("s_ag")
    s_lbt = pair("s_lbt"); s_x0a = pair("s_x0a"); s_x0r = pair("s_x0r")
    s_x1r = pair("s_x1r"); s_tpx = pair("s_tpx"); s_tph = pair("s_tph")
    s_stw = pair("s_stw"); s_str = pair("s_str"); s_x1w = pair("s_x1w")
    s_scw = pair("s_scw"); s_yw = pair("s_yw"); s_ini = sem("s_ini")

    def wseq(eng, pr, n):
        eng.wait_ge(pr[n % 2], 16 * (n // 2 + 1))

    def wseq_tp(eng, n):
        eng.wait_ge(s_tpx[n % 2], 32 * (n // 2 + 1))

    def iseq(ins, pr, n):
        ins.then_inc(pr[n % 2], 16)

    def wupto(eng, pr, n):
        eng.wait_ge(pr[0], 16 * (n // 2 + 1))
        if n >= 1:
            eng.wait_ge(pr[1], 16 * ((n + 1) // 2))

    def wupto_tp(eng, n):
        eng.wait_ge(s_tpx[0], 32 * (n // 2 + 1))
        if n >= 1:
            eng.wait_ge(s_tpx[1], 32 * ((n + 1) // 2))

    def e_pi(p, i):
        return p * NT + i

    def e_wsl(p, h, w):
        return (2 * p + h) * NSLAB + w

    def e_tpx(p, i):
        return p * NT + i

    def wstr_grp(eng, p, h, i):
        grp = (2 * p + h) * NT + i
        eng.wait_ge(s_str[i % 2], 64 * (grp // 2 + 1))

    def wstr_upto(eng, p, h):
        grp = (2 * p + h) * NT + NT - 1
        eng.wait_ge(s_str[1], 64 * (grp // 2 + 1))
        eng.wait_ge(s_str[0], 64 * ((grp - 1) // 2 + 1))
    c_swp = sem("c_swp"); c_ev = sem("c_ev"); c_hsw = sem("c_hsw")
    c_hev = sem("c_hev"); c_gt = sem("c_gt"); c_s0 = sem("c_s0")
    c_v0 = sem("c_v0"); c_r0 = sem("c_r0"); c_ap = sem("c_ap")
    c_vA = sem("c_vA"); c_rA = sem("c_rA"); c_pst = sem("c_pst")
    c_v1 = sem("c_v1"); c_r1 = sem("c_r1"); c_xm = sem("c_xm")
    c_vF = sem("c_vF"); c_rF = sem("c_rF"); c_y2 = sem("c_y2")
    c_msr = sem("c_msr"); c_emm = sem("c_emm"); c_mst = sem("c_mst")
    c_slb = sem("c_slb"); c_gtd = sem("c_gtd"); c_gte = sem("c_gte")
    c_vch = sem("c_vch"); c_sch = sem("c_sch"); c_s1q = sem("c_s1q")
    c_tpw = sem("c_tpw"); c_tpe = sem("c_tpe")
    assert KC % 4 == 0
    RPT = KC // 4
    _nv = {"n": 0}
    _ns = {"n": 0}

    def lhsT(k, i):
        base = (k * NT + i) * 128
        return xfm[:, base:base + 128]

    xfm4 = xfm[:].rearrange("p (k i t) -> p k i t", k=KC, i=NT, t=128)

    def strans_ck(b):
        return strans[b][:, 0:D].rearrange("t (ck k) -> t ck k", ck=KC, k=128)

    with nc.Block() as block:

        @block.gpsimd
        def _(g):
            g.dma_start(emt[:], em_ap).then_inc(s_ini, 16)
            g.dma_start(ident[:], id_ap).then_inc(s_ini, 16)
            for p in range(NP):
                KH = KC // 2 * NW
                for w in range(NSLAB if on(p, PH_AMM) else 0):
                    if p == 0 and not c.nomask and c.mask_mode not in ("r2", "r3", "r4", "r5"):
                        for hh in range(2):
                            if w >= 1:
                                g.wait_ge(c_msr, 2 * (w - 1) + hh + 1)
                            g.dma_start(flex_mask[:, hh * KH:(hh + 1) * KH],
                                        mk_ap[w][:, hh * KH:(hh + 1) * KH]
                                        ).then_inc(s_msl[hh], 16)
                    if p > 0 and w < 2:
                        continue
                    if w >= 2:
                        g.wait_ge(c_swp, v_swp(p, 0, w - 2, NT - 1))
                        if p == 0 and not c.nomask and c.mask_mode not in ("r3", "r4", "r5"):
                            wupto(g, s_scw, w - 2)
                    src = wt_ap[w] if p == 0 else wscr[w]
                    iseq(g.dma_start(wsl[w % 2][:], src), s_wsl, e_wsl(p, 0, w))
                if not on(p, PH_BH):
                    continue
                g.wait_ge(c_swp, v_swp(p, 0, NSLAB - 1, NT - 1))
                if p == 0 and not c.nomask and c.mask_mode not in ("r3", "r4", "r5"):
                    wupto(g, s_scw, NSLAB - 1)
                g.dma_start(wsl[1][:, 0:KC * AGW], ag_ap).then_inc(s_ag, 16)
                for w in range(NSLAB if on(p, PH_BMM) else 0):
                    if w == 1:
                        g.wait_ge(c_hsw, v_pi(p, NT - 1))
                    elif w >= 2:
                        g.wait_ge(c_swp, v_swp(p, 1, w - 2, NT - 1))
                    iseq(g.dma_start(wsl[w % 2][:], bw_ap[w]), s_wsl,
                         e_wsl(p, 1, w))
                    iseq(g.dma_start(lbtw[w % 2][:], lb_ap[w]), s_lbt,
                         p * NSLAB + w)
                if p + 1 < NP and on(p + 1, PH_AMM):
                    for w2 in range(2):
                        g.wait_ge(c_swp,
                                  v_swp(p, 1, NSLAB - 2 + w2, NT - 1))
                        iseq(g.dma_start(wsl[w2][:], wscr[w2]), s_wsl,
                             e_wsl(p + 1, 0, w2))
                for i in range(NT if on(p, PH_BPOST) else 0):
                    g.wait_ge(c_pst, v_pi(p, i))
                    if i >= 2:
                        g.wait_ge(c_y2, v_pi(p, i - 2))
                    wseq(g, s_x1w, e_pi(p, i))
                    iseq(g.dma_start(restile[i % 2][:], x1s[i]), s_x1r,
                         e_pi(p, i))
                    if i >= 1:
                        g.wait_ge(c_y2, v_pi(p, i - 1))
                        iseq(g.dma_start(
                            y_ap[p * TP + (i - 1) * 128:p * TP + i * 128, :],
                            x2out[(i - 1) % 2][:]), s_yw, e_pi(p, i - 1))
                if on(p, PH_BPOST):
                    g.wait_ge(c_y2, v_pi(p, NT - 1))
                    iseq(g.dma_start(
                        y_ap[p * TP + (NT - 1) * 128:p * TP + NT * 128, :],
                        x2out[(NT - 1) % 2][:]), s_yw, e_pi(p, NT - 1))

        @block.sync
        def _(sy):
            def tp_xfm(p, i, b):
                n = e_tpx(p, i)
                if n >= 2:
                    wseq_tp(sy, n - 2)
                KH2 = KC // 2
                sy.dma_start(xfm4[:, 0:KH2, i, :],
                             strans_ck(b)[:, 0:KH2, :],
                             transpose=True).then_inc(s_tpx[n % 2], 16)
                sy.dma_start(xfm4[:, KH2:KC, i, :],
                             strans_ck(b)[:, KH2:KC, :],
                             transpose=True).then_inc(s_tpx[n % 2], 16)

            def stg_reads(p, h, i):
                q = NSLAB // 4
                for g_ in range(4):
                    lo = g_ * q
                    src = stg[lo:lo + q, :, i * NW:(i + 1) * NW].transpose(
                        [1, 0, 2])
                    dst = strans[i % 2][:, lo * NW:(lo + q) * NW]
                    sy.dma_start(dst.rearrange("p (a b) -> p a b", a=q, b=NW),
                                 src).then_inc(s_str[i % 2], 16)

            for p in range(NP):
                if not on(p, PH_APRE):
                    continue
                for i in range(NT):
                    if i >= 2:
                        sy.wait_ge(c_ap, v_pi(p, i - 2))
                        sy.wait_ge(c_s0, v_pi(p, i - 2))
                    elif p > 0:
                        gp = v_pi(p - 1, NT - 2 + (i % 2))
                        sy.wait_ge(c_y2, gp)
                    iseq(sy.dma_start(
                        restile[i % 2][:],
                        x_ap[p * TP + i * 128:p * TP + (i + 1) * 128, :]),
                        s_x0a, e_pi(p, i))
                    if i >= 1:
                        sy.wait_ge(c_ap, v_pi(p, i - 1))
                        if p > 0:
                            sy.wait_ge(c_swp, HB(p, 0))
                            sy.wait_ge(c_hsw, v_pi(p - 1, NT - 1))
                        tp_xfm(p, i - 1, (i - 1) % 2)
                sy.wait_ge(c_ap, v_pi(p, NT - 1))
                if p > 0:
                    sy.wait_ge(c_swp, HB(p, 0))
                    sy.wait_ge(c_hsw, v_pi(p - 1, NT - 1))
                tp_xfm(p, NT - 1, (NT - 1) % 2)
                for w in range(NSLAB if on(p, PH_AMM) else 0):
                    if p == 0 and not c.nomask and c.mask_mode not in ("r3", "r4", "r5"):
                        sy.wait_ge(c_slb, w + 1)
                        iseq(sy.dma_start(wscr[w], wsl[w % 2][:]), s_scw, w)
                    if w == 0 and p > 0:
                        wstr_upto(sy, p - 1, 1)
                    sy.wait_ge(c_ev, v_ev(p, 0, w, NT - 1))
                    iseq(sy.dma_start(stg[w][:, 0:NT * NW],
                                      strans[w % 2][:, 0:NT * NW]),
                         s_stw, e_wsl(p, 0, w))
                for i in range(NT if on(p, PH_APOST) else 0):
                    if i == 0:
                        wupto(sy, s_stw, e_wsl(p, 0, NSLAB - 1))
                    if i >= 2:
                        sy.wait_ge(c_pst, v_pi(p, i - 2))
                        sy.wait_ge(c_xm, v_pi(p, i - 2))
                        if on(p, PH_BH):
                            sy.wait_ge(c_tpe, (p * NT + i - 1) * RPT)
                    stg_reads(p, 0, i)
                    if i < 2:
                        sy.wait_ge(c_ap, v_pi(p, NT - 2 + (i % 2)))
                        sy.wait_ge(c_s0, v_pi(p, NT - 2 + (i % 2)))
                    iseq(sy.dma_start(
                        restile[i % 2][:],
                        x_ap[p * TP + i * 128:p * TP + (i + 1) * 128, :]),
                        s_x0r, e_pi(p, i))
                    sy.wait_ge(c_pst, v_pi(p, i))
                    iseq(sy.dma_start(x1s[i], strans[i % 2][:, 0:D]), s_x1w,
                         e_pi(p, i))
                for i in range(NT if on(p, PH_BH) else 0):
                    sy.wait_ge(c_gt, v_pi(p, i))
                    if e_pi(p, i) >= 2:
                        wseq(sy, s_tph, e_pi(p, i) - 2)
                    iseq(sy.dma_start(
                        hbt.rearrange("p (ck z) -> p ck z", ck=EC, z=TP)[
                            :, :, i * 128:(i + 1) * 128],
                        hfb[:, i * ERP:(i + 1) * ERP].rearrange(
                            "t (ck z) -> t ck z", ck=EC, z=128),
                        transpose=True), s_tph, e_pi(p, i))
                for w in range(NSLAB if on(p, PH_BMM) else 0):
                    if w == 0:
                        wstr_upto(sy, p, 0)
                    sy.wait_ge(c_ev, v_ev(p, 1, w, NT - 1))
                    iseq(sy.dma_start(stg[w][:, 0:NT * NW],
                                      strans[w % 2][:, 0:NT * NW]),
                         s_stw, e_wsl(p, 1, w))
                for i in range(NT if on(p, PH_BPOST) else 0):
                    if i == 0:
                        wupto(sy, s_stw, e_wsl(p, 1, NSLAB - 1))
                    if i >= 2:
                        sy.wait_ge(c_y2, v_pi(p, i - 2))
                    stg_reads(p, 1, i)
            lp, lph = lim
            if lph >= PH_BPOST:
                wupto(sy, s_yw, lp * NT + NT - 1)
                if "dbg" in io:
                    sy.dma_start(io["dbg"], stats[:]).then_inc(s_ini, 16)
                    sy.wait_ge(s_ini, 48)
            elif lph >= PH_BMM:
                wupto(sy, s_stw, e_wsl(lp, 1, NSLAB - 1))
            elif lph >= PH_BH:
                wupto(sy, s_tph, lp * NT + NT - 1)
            elif lph >= PH_APOST:
                wupto(sy, s_x1w, lp * NT + NT - 1)
            elif lph >= PH_AMM:
                wupto(sy, s_stw, e_wsl(lp, 0, NSLAB - 1))
            else:
                wupto_tp(sy, e_tpx(lp, NT - 1))

        @block.tensor
        def _(t):
            def emm(we):
                B0 = (we % 2) * KC * NB
                for ck in range(KC):
                    ins_ = t.matmul(
                        pbE[:, 256 + ck * NB:256 + (ck + 1) * NB],
                        emt[:], bms[:, B0 + ck * NB:B0 + (ck + 1) * NB],
                        start=True, stop=True)
                return ins_

            _tp0 = {"first": True}
            for p in range(NP):
                if not on(p, PH_AMM):
                    continue
                if p == 0 and not c.nomask and c.mask_mode not in ("red", "r1", "r2", "r3", "r4", "r5"):
                    t.wait_ge(s_ini, 32)
                    t.wait_ge(c_msr, 2)
                    emm(0).then_inc(c_emm, 1)
                for w in range(NSLAB):
                    for i in range(NT):
                        if w == 0:
                            wseq_tp(t, e_tpx(p, i))
                        if p == 0 and not c.nomask and c.mask_mode != "r5":
                            t.wait_ge(c_slb, w + 1)
                        else:
                            wseq(t, s_wsl, e_wsl(p, 0, w))
                        if i >= NBANK:
                            t.wait_ge(c_ev, v_ev(p, 0, w, i - NBANK))
                        elif w >= 1:
                            t.wait_ge(c_ev, v_ev(p, 0, w - 1, bank_last(i)))
                        elif HB(p, 0) > 0:
                            t.wait_ge(c_ev, HB(p, 0))
                        for k in range(KC):
                            ins_ = t.matmul(pb(i)[:, 0:NW], lhsT(k, i),
                                            wsl[w % 2][:, k * NW:(k + 1) * NW],
                                            start=(k == 0), stop=(k == KC - 1))
                        ins_.then_inc(c_swp, 1)
                        if (p == 0 and not c.nomask and i == 0
                                and w + 1 < NSLAB and c.mask_mode not in ("red", "r1", "r2", "r3", "r4", "r5")):
                            we = w + 1
                            t.wait_ge(c_msr, 2 * (we + 1))
                            t.wait_ge(c_mst, we)
                            emm(we).then_inc(c_emm, 1)
                def h_sweep(i):
                    t.wait_ge(s_ag, 16 * (p + 1))
                    t.wait_ge(c_tpe, (p * NT + i + 1) * RPT)
                    if i >= NBANK:
                        t.wait_ge(c_ev, v_ev(p, 1, 0, i - NBANK))
                    else:
                        t.wait_ge(c_ev, v_ev(p, 0, NSLAB - 1, bank_last(i)))
                    for k in range(KC):
                        ins_ = t.matmul(pb(i)[:, 256:256 + AGW], lhsT(k, i),
                                        wsl[1][:, k * AGW:(k + 1) * AGW],
                                        start=(k == 0), stop=(k == KC - 1))
                    ins_.then_inc(c_hsw, 1)

                def base0(i):
                    wseq(t, s_wsl, e_wsl(p, 1, 0))
                    t.wait_ge(c_hev, v_pi(p, i))
                    for k in range(KC):
                        t.matmul(pb(i)[:, 0:NW], lhsT(k, i),
                                 wsl[0][:, k * NW:(k + 1) * NW],
                                 start=(k == 0), stop=False)

                def lora0(i):
                    wseq(t, s_lbt, p * NSLAB + 0)
                    wseq(t, s_tph, e_pi(p, i))
                    for ck in range(EC):
                        ins_ = t.matmul(
                            pb(i)[:, 0:NW],
                            hbt[:, ck * TP + i * 128:ck * TP + (i + 1) * 128],
                            lbtw[0][:, ck * NW:(ck + 1) * NW],
                            start=False, stop=(ck == EC - 1))
                    ins_.then_inc(c_swp, 1)

                def tp_pe(i):
                    gbase = (p * NT + i) * RPT
                    t.wait_ge(c_xm, v_pi(p, i))
                    if _tp0["first"]:
                        t.wait_ge(s_ini, 32)
                        _tp0["first"] = False
                    if i >= NBANK and on(p, PH_BMM):
                        t.wait_ge(c_ev, v_ev(p, 1, 0, i - NBANK))
                    else:
                        t.wait_ge(c_ev, v_ev(p, 0, NSLAB - 1, bank_last(i)))
                    for r in range(RPT):
                        g_ = gbase + r
                        if g_ >= 2:
                            t.wait_ge(c_tpe, g_ - 1)
                        bank = pbM if r % 2 == 0 else pb(i)
                        for cc in range(4):
                            k = r * 4 + cc
                            ins_ = t.matmul(
                                bank[:, cc * 128:(cc + 1) * 128],
                                strans[i % 2][:, k * 128:(k + 1) * 128],
                                ident[:], start=True, stop=True)
                        ins_.then_inc(c_tpw, 1)

                if on(p, PH_BH):
                    for i in range(NT):
                        if i >= NBANK and on(p, PH_BMM):
                            lora0(i - NBANK)
                        tp_pe(i)
                        h_sweep(i)
                        if i >= 1 and on(p, PH_BMM):
                            base0(i - 1)
                    if on(p, PH_BMM):
                        base0(NT - 1)
                        for j in range(max(NT - NBANK, 0), NT):
                            lora0(j)
                for w in (range(1, NSLAB) if on(p, PH_BMM) else []):
                    for i in range(NT):
                        wseq(t, s_wsl, e_wsl(p, 1, w))
                        wseq(t, s_lbt, p * NSLAB + w)
                        if i >= NBANK:
                            t.wait_ge(c_ev, v_ev(p, 1, w, i - NBANK))
                        else:
                            t.wait_ge(c_ev, v_ev(p, 1, w - 1, bank_last(i)))
                        for k in range(KC):
                            t.matmul(pb(i)[:, 0:NW], lhsT(k, i),
                                     wsl[w % 2][:, k * NW:(k + 1) * NW],
                                     start=(k == 0), stop=False)
                        for ck in range(EC):
                            ins_ = t.matmul(
                                pb(i)[:, 0:NW],
                                hbt[:, ck * TP + i * 128:ck * TP + (i + 1) * 128],
                                lbtw[w % 2][:, ck * NW:(ck + 1) * NW],
                                start=False, stop=(ck == EC - 1))
                        ins_.then_inc(c_swp, 1)

        @block.scalar
        def _(sc):
            _ne = {"n": 0}

            def rstd_chain(i, dst, done_sem):
                sc.activation(scol("tln", i), scol("tvar", i), AF.Ln,
                              bias=consts[:, 0:1]).then_inc(c_sch, 1)
                _ns["n"] += 1
                sc.wait_ge(c_sch, _ns["n"])
                sc.activation(scol(dst, i), scol("tln", i), AF.Exp,
                              scale=consts[:, 1:2]).then_inc(done_sem, 1)

            for p in range(NP):
                for i in range(NT if on(p, PH_APRE) else 0):
                    wseq(sc, s_x0a, e_pi(p, i))
                    if i >= 2:
                        wseq_tp(sc, e_tpx(p, i - 2))
                    sc.activation(strans[i % 2][:, 0:D], restile[i % 2][:], AF.Square,
                                  accum_out=scol("sq0", i)).then_inc(c_s0, 1)
                    sc.wait_ge(c_v0, v_pi(p, i))
                    rstd_chain(i, "r0", c_r0)
                for w in range(NSLAB if on(p, PH_AMM) else 0):
                    for i in range(NT):
                        sc.wait_ge(c_swp, v_swp(p, 0, w, i))
                        if w == 0 and i == 0:
                            wupto_tp(sc, e_tpx(p, NT - 1))
                        if w >= 2 and i == 0:
                            wseq(sc, s_stw, e_wsl(p, 0, w - 2))
                        sc.activation(strans[w % 2][:, i * NW:(i + 1) * NW],
                                      pb(i)[:, 0:NW], AF.Copy,
                                      accum_out=scol("acol", i * NSLAB + w))
                        sc.wait_ge(c_ev, _ne["n"])
                        sc.activation(dump[:], pb(i)[:, 0:NW], AF.Square,
                                      accum_out=scol("qcol", i * NSLAB + w)
                                      ).then_inc(c_ev, 1)
                        _ne["n"] += 1
                def evacB(w, i):
                    sc.wait_ge(c_swp, v_swp(p, 1, w, i))
                    if w == 0 and i == 0:
                        sc.wait_ge(c_tpe, (p * NT + NT - 1) * RPT)
                    if w == 1 and i == 0:
                        sc.wait_ge(c_tpe, (p * NT + NT) * RPT)
                    if w >= 2 and i == 0:
                        wseq(sc, s_stw, e_wsl(p, 1, w - 2))
                    sc.activation(strans[w % 2][:, i * NW:(i + 1) * NW],
                                  pb(i)[:, 0:NW], AF.Copy,
                                  accum_out=scol("acol", i * NSLAB + w))
                    sc.wait_ge(c_ev, _ne["n"])
                    sc.activation(dump[:], pb(i)[:, 0:NW], AF.Square,
                                  accum_out=scol("qcol", i * NSLAB + w)
                                  ).then_inc(c_ev, 1)
                    _ne["n"] += 1

                def tp_evac(i):
                    gbase = (p * NT + i) * RPT
                    for r in range(RPT):
                        sc.wait_ge(c_tpw, gbase + r + 1)
                        bank = pbM if r % 2 == 0 else pb(i)
                        sc.activation(xfm4[:, 4 * r:4 * r + 4, i, :],
                                      bank[:].rearrange("p (a b) -> p a b",
                                                        a=4, b=128),
                                      AF.Copy).then_inc(c_tpe, 1)

                def hev_gate(j):
                    sc.wait_ge(c_hsw, v_pi(p, j))
                    sc.activation(hraw[:, j * AGW:(j + 1) * AGW],
                                  pb(j)[:, 256:256 + AGW], AF.Copy
                                  ).then_inc(c_hev, 1)
                    sc.wait_ge(c_gtd, v_pi(p, j))
                    sc.activation(scol("ecol"), scol("dcol"), AF.Exp
                                  ).then_inc(c_gte, 1)

                for i in range(NT if on(p, PH_APOST) else 0):
                    if i == 0:
                        sc.wait_ge(c_vA, v_pi(p, 0))
                        rstd_chain(0, "rA", c_rA)
                    sc.wait_ge(c_pst, v_pi(p, i))
                    sc.activation(x2out[i % 2][:], strans[i % 2][:, 0:D],
                                  AF.Square,
                                  accum_out=scol("x1sq", i)).then_inc(c_s1q, 1)
                    if i >= 1 and on(p, PH_BH):
                        if i - 1 >= NBANK and on(p, PH_BMM):
                            evacB(0, i - 1 - NBANK)
                        hev_gate(i - 1)
                    sc.wait_ge(c_v1, v_pi(p, i))
                    rstd_chain(i, "r1", c_r1)
                    if i + 1 < NT:
                        sc.wait_ge(c_vA, v_pi(p, i + 1))
                        rstd_chain(i + 1, "rA", c_rA)
                    if on(p, PH_BH) and (i < NT - 1 or NT - 1 < NBANK):
                        tp_evac(i)
                if on(p, PH_BH):
                    if NT - 1 >= NBANK:
                        if on(p, PH_BMM):
                            evacB(0, NT - 1 - NBANK)
                        tp_evac(NT - 1)
                    hev_gate(NT - 1)
                if on(p, PH_BH) and on(p, PH_BMM):
                    for j in range(max(NT - NBANK, 0), NT):
                        evacB(0, j)
                for w in (range(1, NSLAB) if on(p, PH_BMM) else []):
                    for i in range(NT):
                        evacB(w, i)
                for i in range(NT if on(p, PH_BPOST) else 0):
                    sc.wait_ge(c_vF, v_pi(p, i))
                    rstd_chain(i, "rA", c_rF)

        @block.vector
        def _(v):
            def vf(ins):
                ins.then_inc(c_vch, 1)
                _nv["n"] += 1

            def vw():
                v.wait_ge(c_vch, _nv["n"])

            def var_chain(sumc, sqc, muc, i):
                vw()
                vf(v.tensor_scalar(scol(muc, i), scol(sumc, i), rD, None,
                                   ALU.mult))
                vw()
                vf(v.tensor_tensor(scol("tm2", i), scol(muc, i), scol(muc, i),
                                   ALU.mult))
                vw()
                return v.scalar_tensor_tensor(scol("tvar", i), scol(sqc, i), rD,
                                              scol("tm2", i), ALU.mult,
                                              ALU.subtract)

            v.memset(consts[:, 0:1], float(c.eps))
            v.memset(consts[:, 1:2], -0.5)
            for p in range(NP):
                state = {"mask_w": 0, "prep_w": 0}

                def emit_mask_prep():
                    w = state["prep_w"]
                    B0 = (w % 2) * KC * NB
                    if c.mask_mode in ("r1", "r2", "r3", "r4", "r5"):
                        v.memset(bms[:, B0:B0 + KC * NB], 0.0).then_inc(c_msr, 1)
                        v.memset(bms[:, B0:B0 + KC * NB], 0.0).then_inc(c_msr, 1)
                    else:
                        KH2 = KC // 2 * NW
                        BH2 = KC // 2 * NB
                        if w >= 2:
                            v.wait_ge(c_emm, w - 1)
                        for hh in range(2):
                            v.wait_ge(s_msl[hh], 16 * (w + 1))
                            with nc.allow_low_precision(
                                    reason="mask 0/1 block sums <=32 exact"):
                                v.tensor_reduce(
                                    bms[:, B0 + hh * BH2:B0 + (hh + 1) * BH2],
                                    flex_mask[:, hh * KH2:(hh + 1) * KH2
                                              ].rearrange(
                                        "p (ck b z) -> p ck b z", ck=KC // 2,
                                        b=NB, z=32),
                                    AX.X, ALU.add).then_inc(c_msr, 1)
                    state["prep_w"] += 1

                def emit_mask_mult():
                    w = state["mask_w"]
                    if c.mask_mode not in ("red", "r1", "r2", "r3", "r4", "r5"):
                        v.wait_ge(c_emm, w + 1)
                        v.tensor_scalar(pexp_all[:, w * KC * NB:(w + 1) * KC * NB],
                                        pbE[:, 256:256 + KC * NB], 1.0,
                                        None, ALU.min).then_inc(c_mst, 1)
                    wseq(v, s_wsl, e_wsl(p, 0, w))
                    if c.mask_mode == "full":
                        v.wait_ge(c_mst, w + 1)
                        v4 = wsl[w % 2][:].rearrange("p (ck b z) -> p ck b z",
                                                     ck=KC, b=NB, z=32)
                        v.tensor_tensor(
                            v4, v4,
                            pexp_all[:, w * KC * NB:(w + 1) * KC * NB
                                     ].rearrange("p (ck b) -> p ck b", ck=KC,
                                                 b=NB).unsqueeze(3
                                                 ).to_broadcast(
                                                     [128, KC, NB, 32]),
                            ALU.mult).then_inc(c_slb, 1)
                    elif c.mask_mode in ("r4", "r5"):
                        v.memset(bms[:], 1.0).then_inc(c_slb, 1)
                    else:
                        v.tensor_scalar(wsl[w % 2][:], wsl[w % 2][:], 1.0,
                                        None, ALU.mult).then_inc(c_slb, 1)
                    state["mask_w"] += 1

                for i in range(NT if on(p, PH_APRE) else 0):
                    wseq(v, s_x0a, e_pi(p, i))
                    vf(v.tensor_reduce(scol("sum0", i), restile[i % 2][:], AX.X,
                                       ALU.add))
                    v.wait_ge(c_s0, v_pi(p, i))
                    var_chain("sum0", "sq0", "mu0", i).then_inc(c_v0, 1)
                    v.wait_ge(c_r0, v_pi(p, i))
                    v.tensor_scalar(strans[i % 2][:, 0:D], restile[i % 2][:],
                                    scol("mu0", i), scol("r0", i),
                                    ALU.subtract, ALU.mult).then_inc(c_ap, 1)
                    if p == 0 and on(p, PH_AMM) and not c.nomask:
                        while state["prep_w"] < min(i + 2, 3, NSLAB):
                            emit_mask_prep()
                        while state["mask_w"] < min(i + 1, 2):
                            emit_mask_mult()
                if p == 0 and on(p, PH_AMM) and not c.nomask:
                    while state["mask_w"] < NSLAB or state["prep_w"] < NSLAB:
                        if state["prep_w"] < min(state["mask_w"] + 1, NSLAB):
                            emit_mask_prep()
                        else:
                            emit_mask_mult()
                def _gating_body(i):
                    lg = hraw[:, i * AGW + ER:(i + 1) * AGW]
                    vw()
                    vf(v.max(out=stats[:, off["m8"]:off["m8"] + 8], in_=lg))
                    vw()
                    vf(v.memset(stats[:, off["m8"] + 2:off["m8"] + 8], NEG_BIG))
                    vw()
                    vf(v.match_replace(out=scol("zap2", 0, NE),
                                       in_to_replace=stats[:, off["m8"]:off["m8"] + 8],
                                       in_values=lg, imm_value=NEG_BIG))
                    vw()
                    vf(v.tensor_copy(stats[:, off["m8b"]:off["m8b"] + 8],
                                     stats[:, off["m8"]:off["m8"] + 8]))
                    vw()
                    vf(v.memset(stats[:, off["m8b"] + 1:off["m8b"] + 8],
                                NEG_BIG))
                    vw()
                    vf(v.match_replace(out=scol("zap1", 0, NE),
                                       in_to_replace=stats[:, off["m8b"]:off["m8b"] + 8],
                                       in_values=lg, imm_value=NEG_BIG))
                    vw()
                    vf(v.tensor_tensor(scol("ind1", 0, NE), scol("zap1", 0, NE),
                                       lg, ALU.not_equal))
                    vf(v.tensor_tensor(scol("ind12", 0, NE),
                                       scol("zap2", 0, NE), lg, ALU.not_equal))
                    v.tensor_tensor(scol("dcol"),
                                    stats[:, off["m8"] + 1:off["m8"] + 2],
                                    stats[:, off["m8"]:off["m8"] + 1],
                                    ALU.subtract).then_inc(c_gtd, 1)
                    v.wait_ge(c_gte, v_pi(p, i))
                    vf(v.tensor_scalar(scol("opl"), scol("ecol"), 1.0, None,
                                       ALU.add))
                    vw()
                    vf(v.reciprocal(scol("p1"), scol("opl")))
                    vw()
                    vf(v.tensor_tensor(scol("p2"), scol("ecol"), scol("p1"),
                                       ALU.mult))
                    vw()
                    vf(v.tensor_tensor(scol("t2"), scol("p1"), scol("p2"),
                                       ALU.subtract))
                    vw()
                    vf(v.tensor_scalar(scol("c1"), scol("t2"), c.SCALING, None,
                                       ALU.mult))
                    vf(v.tensor_scalar(scol("c2"), scol("p2"), c.SCALING, None,
                                       ALU.mult))
                    vw()
                    vf(v.tensor_scalar(scol("t3", 0, NE), scol("ind12", 0, NE),
                                       scol("c2"), None, ALU.mult))
                    vw()
                    vf(v.scalar_tensor_tensor(scol("ind1", 0, NE),
                                              scol("ind1", 0, NE), scol("c1"),
                                              scol("t3", 0, NE), ALU.mult,
                                              ALU.add))
                    vw()
                    v.tensor_tensor(
                        hfb[:, i * ERP:i * ERP + ER].rearrange(
                            "p (e r) -> p e r", e=NE, r=c.R),
                        hraw[:, i * AGW:i * AGW + ER].rearrange(
                            "p (e r) -> p e r", e=NE, r=c.R),
                        scol("ind1", 0, NE).unsqueeze(2).to_broadcast(
                            [128, NE, c.R]),
                        ALU.mult).then_inc(c_gt, 1)

                def emit_gating(i):
                    if not on(p, PH_BH):
                        return
                    v.wait_ge(c_hev, v_pi(p, i))
                    _gating_body(i)

                if on(p, PH_BH) and on(p, PH_APOST):
                    vf(v.memset(hfb[:], 0.0))
                def reds_varA(i):
                    v.wait_ge(c_ev, v_ev(p, 0, NSLAB - 1, NT - 1))
                    v.tensor_reduce(scol("asum", i),
                                    scol("acol", i * NSLAB, NSLAB), AX.X,
                                    ALU.add)
                    vf(v.tensor_reduce(scol("qsum", i),
                                       scol("qcol", i * NSLAB, NSLAB), AX.X,
                                       ALU.add))
                    var_chain("asum", "qsum", "muA", i).then_inc(c_vA, 1)

                for i in range(NT if on(p, PH_APOST) else 0):
                    if i == 0:
                        reds_varA(0)
                    v.wait_ge(c_rA, v_pi(p, i))
                    wstr_grp(v, p, 0, i)
                    wseq(v, s_x0r, e_pi(p, i))
                    if p > 0 and i < 2:
                        wseq(v, s_yw, e_pi(p - 1, NT - 2 + (i % 2)))
                    if i >= 2:
                        v.wait_ge(c_s1q, v_pi(p, i - 2))
                    vf(v.tensor_scalar(x2out[i % 2][:], strans[i % 2][:, 0:D],
                                       scol("muA", i), scol("rA", i),
                                       ALU.subtract, ALU.mult))
                    vw()
                    v.scalar_tensor_tensor(strans[i % 2][:, 0:D],
                                           x2out[i % 2][:],
                                           1.0, restile[i % 2][:], ALU.mult,
                                           ALU.add,
                                           accum_out=scol("x1sum", i)
                                           ).then_inc(c_pst, 1)
                    v.wait_ge(c_pst, v_pi(p, i))
                    if i >= 1:
                        emit_gating(i - 1)
                    v.wait_ge(c_s1q, v_pi(p, i))
                    var_chain("x1sum", "x1sq", "mu1", i).then_inc(c_v1, 1)
                    v.wait_ge(c_r1, v_pi(p, i))
                    wseq(v, s_x1w, e_pi(p, i))
                    v.tensor_scalar(strans[i % 2][:, 0:D],
                                    strans[i % 2][:, 0:D],
                                    scol("mu1", i), scol("r1", i),
                                    ALU.subtract, ALU.mult).then_inc(c_xm, 1)
                    if i + 1 < NT:
                        reds_varA(i + 1)
                if on(p, PH_BH):
                    emit_gating(NT - 1)
                for i in range(NT if on(p, PH_BPOST) else 0):
                    v.wait_ge(c_ev, v_ev(p, 1, NSLAB - 1, NT - 1))
                    v.tensor_reduce(scol("asum", i),
                                    scol("acol", i * NSLAB, NSLAB), AX.X,
                                    ALU.add)
                    vf(v.tensor_reduce(scol("qsum", i),
                                       scol("qcol", i * NSLAB, NSLAB), AX.X,
                                       ALU.add))
                    var_chain("asum", "qsum", "muA", i).then_inc(c_vF, 1)
                    v.wait_ge(c_rF, v_pi(p, i))
                    wstr_grp(v, p, 1, i)
                    wseq(v, s_x1r, e_pi(p, i))
                    if i >= 2:
                        wseq(v, s_yw, e_pi(p, i - 2))
                    vf(v.tensor_scalar(x2out[i % 2][:], strans[i % 2][:, 0:D],
                                       scol("muA", i), scol("rA", i),
                                       ALU.subtract, ALU.mult))
                    vw()
                    v.scalar_tensor_tensor(x2out[i % 2][:], x2out[i % 2][:], 1.0,
                                           restile[i % 2][:], ALU.mult, ALU.add
                                           ).then_inc(c_y2, 1)

    st_.close()
    return nc


def _run(inputs, c: Cfg, n_cores=8, lim=None):
    T = c.NPASS * c.TP
    x = np.ascontiguousarray(
        np.asarray(inputs["L0_attn_data"], dtype=np.float32).reshape(-1, c.D))
    host = host_layouts(
        c,
        np.asarray(inputs["L0_attn_weight"], dtype=np.float32),
        np.asarray(inputs["L0_attn_mask"], dtype=np.int32),
        np.asarray(inputs["L0_ffn_base_W"], dtype=np.float32),
        np.asarray(inputs["L0_ffn_gate_W"], dtype=np.float32),
        np.asarray(inputs["L0_ffn_lora_A"], dtype=np.float32),
        np.asarray(inputs["L0_ffn_lora_B"], dtype=np.float32),
    )
    nc = bass.Bass("TRN2", target_bir_lowering=False)
    io = {}
    io["x"] = nc.dram_tensor("x", [T, c.D], F32, kind="ExternalInput").ap()
    io["y"] = nc.dram_tensor("y", [T, c.D], F32, kind="ExternalOutput").ap()
    for nm, dt_ in [("wt", F32), ("mk", I32), ("bw", F32), ("ag", F32),
                    ("lb", F32)]:
        io[nm] = nc.dram_tensor(nm, list(host[nm].shape), dt_,
                                kind="ExternalInput").ap()
    io["em"] = nc.dram_tensor("em", [128, 128], BF16, kind="ExternalInput").ap()
    io["id"] = nc.dram_tensor("id", [128, 128], BF16, kind="ExternalInput").ap()
    if getattr(c, "debug_stats", False):
        io["dbg"] = nc.dram_tensor("dbg", [128, 1024], F32,
                                   kind="ExternalOutput").ap()
    build(nc, c, io, lim=lim)
    in_maps = []
    for ci in range(n_cores):
        m = dict(host)
        m["x"] = np.ascontiguousarray(x[ci * T:(ci + 1) * T])
        in_maps.append(m)
    res = run_bass_kernel_spmd(nc, in_maps, core_ids=list(range(n_cores)))
    out = np.concatenate([r["y"] for r in res.results], axis=0)
    if getattr(c, "debug_stats", False):
        return out.reshape(1, -1, c.D), [r["dbg"] for r in res.results]
    return out.reshape(1, -1, c.D)


def kernel(L0_attn_data, L0_attn_weight, L0_attn_mask, L0_ffn_base_W,
           L0_ffn_gate_W, L0_ffn_lora_A, L0_ffn_lora_B):
    c = Cfg()
    return _run(dict(
        L0_attn_data=L0_attn_data, L0_attn_weight=L0_attn_weight,
        L0_attn_mask=L0_attn_mask, L0_ffn_base_W=L0_ffn_base_W,
        L0_ffn_gate_W=L0_ffn_gate_W, L0_ffn_lora_A=L0_ffn_lora_A,
        L0_ffn_lora_B=L0_ffn_lora_B), c)


# version 18
# speedup vs baseline: 1.0560x; 1.0226x over previous
import numpy as np
import ml_dtypes
import concourse.bass as bass
import concourse.mybir as mybir
from concourse.bass_utils import run_bass_kernel_spmd

F32 = mybir.dt.float32
BF16 = mybir.dt.bfloat16
I32 = mybir.dt.int32
AF = mybir.ActivationFunctionType
ALU = mybir.AluOpType
AX = mybir.AxisListType

NEG_BIG = -1e30


class Cfg:
    def __init__(self, D=4096, TP=1024, NPASS=2, NW=256, NE=28, R=8, eps=1e-5):
        self.D = D
        self.TP = TP
        self.NPASS = NPASS
        self.NW = NW
        self.NE = NE
        self.R = R
        self.eps = eps
        self.KC = D // 128
        self.NT = TP // 128
        self.NSLAB = D // NW
        self.ER = NE * R
        self.ERP = 256
        self.EC = self.ERP // 128
        self.AGW = self.ER + NE
        self.NB = NW // 32
        assert D % 128 == 0 and TP % 128 == 0 and NW % 32 == 0 and D % NW == 0
        assert self.NSLAB % 4 == 0 and self.NT >= 2
        assert self.ER <= self.ERP <= 2 * 128 and self.AGW <= 256
        assert self.AGW <= self.NW
        self.SCALING = 2.0
        self.nomask = False
        self.mask_mode = "full"


def host_layouts(c: Cfg, W, mask, baseW, gateW, loraA, loraB):
    D, NW, NSLAB, KC = c.D, c.NW, c.NSLAB, c.KC

    def pm(a):
        return np.ascontiguousarray(
            a.reshape(KC, 128, NSLAB, NW).transpose(2, 1, 0, 3)
        ).reshape(NSLAB, 128, KC * NW)

    wt = pm(np.ascontiguousarray(W.T))
    mk = pm(np.ascontiguousarray(mask.T))
    bw = pm(np.ascontiguousarray(baseW.T))
    ag_f = np.concatenate([loraA.reshape(c.ER, D).T, gateW.T], axis=1)
    ag = np.ascontiguousarray(
        ag_f.reshape(KC, 128, c.AGW).transpose(1, 0, 2)).reshape(128, KC * c.AGW)
    lb_f = np.zeros((c.ERP, D), dtype=np.float32)
    lb_f[:c.ER] = loraB.transpose(0, 2, 1).reshape(c.ER, D)
    lb = np.ascontiguousarray(
        lb_f.reshape(c.EC, 128, NSLAB, NW).transpose(2, 1, 0, 3)
    ).reshape(NSLAB, 128, c.EC * NW)
    em = (np.arange(128)[:, None] // 32 == np.arange(128)[None, :] // 32)
    em = np.asarray(em.astype(ml_dtypes.bfloat16))
    idm = np.asarray(np.eye(128).astype(ml_dtypes.bfloat16))
    return dict(wt=wt.astype(np.float32), mk=mk.astype(np.int32),
                bw=bw.astype(np.float32), ag=ag.astype(np.float32),
                lb=lb.astype(np.float32), em=em, id=idm)


PH_APRE, PH_AMM, PH_APOST, PH_BH, PH_BMM, PH_BPOST = range(6)


def build(nc: bass.Bass, c: Cfg, io, lim=None):
    if lim is None:
        lim = (c.NPASS - 1, PH_BPOST)

    def on(p, ph):
        return (p, ph) <= lim

    D, TP, NP, NW = c.D, c.TP, c.NPASS, c.NW
    KC, NT, NSLAB, NB = c.KC, c.NT, c.NSLAB, c.NB
    ER, ERP, EC, AGW = c.ER, c.ERP, c.EC, c.AGW
    NE = c.NE
    rD = 1.0 / D

    x_ap, y_ap = io["x"], io["y"]
    wt_ap, mk_ap, bw_ap = io["wt"], io["mk"], io["bw"]
    ag_ap, lb_ap, em_ap = io["ag"], io["lb"], io["em"]
    id_ap = io["id"]

    wscr = nc.dram_tensor("wscr", [NSLAB, 128, KC * NW], BF16).ap()
    stg = nc.dram_tensor("stg", [NSLAB, 128, NT * NW], BF16).ap()
    x1s = nc.dram_tensor("x1s", [NT, 128, D], BF16).ap()

    def HB(p, h):
        return (2 * p + h) * NSLAB * NT

    def v_swp(p, h, w, i):
        return HB(p, h) + w * NT + i + 1

    v_ev = v_swp

    def v_pi(p, i):
        return p * NT + i + 1

    def v_wsl(p, h, w):
        return 16 * ((2 * p + h) * NSLAB + w + 1)

    def v_tpx(p, h, i):
        return 16 * (p * 2 * NT + h * NT + i + 1)

    def v_stw(p, h, w):
        return 16 * ((2 * p + h) * NSLAB + w + 1)

    def v_str(p, h, i, g):
        return 16 * (((2 * p + h) * NT + i) * 4 + g + 1)

    off = {}
    cur = 0

    def alloc_cols(name, n):
        nonlocal cur
        off[name] = cur
        cur += n

    alloc_cols("sum0", NT); alloc_cols("sq0", NT)
    alloc_cols("mu0", NT); alloc_cols("r0", NT)
    alloc_cols("acol", NT * NSLAB); alloc_cols("qcol", NT * NSLAB)
    alloc_cols("asum", NT); alloc_cols("qsum", NT)
    alloc_cols("muA", NT); alloc_cols("rA", NT)
    alloc_cols("x1sum", NT); alloc_cols("x1sq", NT)
    alloc_cols("mu1", NT); alloc_cols("r1", NT)
    alloc_cols("tvar", NT); alloc_cols("tln", NT); alloc_cols("tm2", NT)
    alloc_cols("m8", 8); alloc_cols("m8b", 8)
    alloc_cols("zap1", NE); alloc_cols("zap2", NE)
    alloc_cols("ind1", NE); alloc_cols("ind12", NE)
    alloc_cols("dcol", 1); alloc_cols("ecol", 1); alloc_cols("opl", 1)
    alloc_cols("p1", 1); alloc_cols("p2", 1); alloc_cols("t2", 1)
    alloc_cols("c1", 1); alloc_cols("c2", 1); alloc_cols("t3", NE)
    SCOLS = cur
    assert SCOLS <= 1024

    from contextlib import ExitStack
    st_ = ExitStack()
    sb = lambda name, shape, dt: st_.enter_context(nc.sbuf_tensor(name, shape, dt))
    ps = lambda name, shape: st_.enter_context(nc.psum_tensor(name, shape, F32))
    sem = lambda name: st_.enter_context(nc.semaphore(name))

    xfm = sb("xfm", [128, KC * TP], BF16)
    wsl = [sb(f"wsl{b}", [128, KC * NW], BF16) for b in range(2)]
    FLEXN = max(2 * D, 2 * NT * AGW + NT * ERP + EC * TP, KC * NW)
    flex = sb("flex", [128, FLEXN], BF16)
    STRN = max(D, NT * NW)
    strans = [sb(f"strans{b}", [128, STRN], BF16) for b in range(2)]
    restile = [sb(f"restile{b}", [128, D], F32) for b in range(2)]
    x2out = [sb(f"x2out{b}", [128, D], F32) for b in range(2)]
    stats = sb("stats", [128, SCOLS if not getattr(c, "debug_stats", False) else 1024], F32)
    pexp_all = sb("pexp_all", [128, NSLAB * KC * NB], BF16)
    bms = sb("bms", [128, 2 * KC * NB], BF16)
    lbtw = [sb(f"lbtw{b}", [128, EC * NW], BF16) for b in range(2)]
    emt = sb("emt", [128, 128], BF16)
    ident = sb("ident", [128, 128], BF16)
    dump = sb("dump", [128, NW], BF16)
    consts = sb("consts", [128, 2], F32)

    NBANK = min(NT, 7)
    pb_ = [ps(f"pb{i}", [128, 512]) for i in range(NBANK)]
    pbM = ps("pbM", [128, 512])

    def pb(i):
        return pb_[i % NBANK]

    def bank_last(i):
        b = i % NBANK
        return b + NBANK if b + NBANK < NT else b
    pbE = pbM

    def flex_x1(b):
        return flex[:, b * D:(b + 1) * D]

    flex_mask = flex[:, 0:KC * NW]
    hraw_cols = NT * AGW
    hraw = flex[:, 0:2 * hraw_cols].bitcast(F32)
    hfb_base = 2 * hraw_cols
    hfb = flex[:, hfb_base:hfb_base + NT * ERP]
    hbt_base = hfb_base + NT * ERP
    hbt = flex[:, hbt_base:hbt_base + EC * TP]
    assert hbt_base + EC * TP <= FLEXN

    def scol(name, i=0, n=1):
        o = off[name] + i
        return stats[:, o:o + n]

    pair = lambda n: [sem(n + "a"), sem(n + "b")]
    s_wsl = pair("s_wsl"); s_msl = pair("s_msl"); s_ag = sem("s_ag")
    s_lbt = pair("s_lbt"); s_x0a = pair("s_x0a"); s_x0r = pair("s_x0r")
    s_x1r = pair("s_x1r"); s_tpx = pair("s_tpx"); s_tph = pair("s_tph")
    s_stw = pair("s_stw"); s_str = pair("s_str"); s_x1w = pair("s_x1w")
    s_scw = pair("s_scw"); s_yw = pair("s_yw"); s_ini = sem("s_ini")

    def wseq(eng, pr, n):
        eng.wait_ge(pr[n % 2], 16 * (n // 2 + 1))

    def wseq_tp(eng, n):
        eng.wait_ge(s_tpx[n % 2], 32 * (n // 2 + 1))

    def iseq(ins, pr, n):
        ins.then_inc(pr[n % 2], 16)

    def wupto(eng, pr, n):
        eng.wait_ge(pr[0], 16 * (n // 2 + 1))
        if n >= 1:
            eng.wait_ge(pr[1], 16 * ((n + 1) // 2))

    def wupto_tp(eng, n):
        eng.wait_ge(s_tpx[0], 32 * (n // 2 + 1))
        if n >= 1:
            eng.wait_ge(s_tpx[1], 32 * ((n + 1) // 2))

    def e_pi(p, i):
        return p * NT + i

    def e_wsl(p, h, w):
        return (2 * p + h) * NSLAB + w

    def e_tpx(p, i):
        return p * NT + i

    def wstr_grp(eng, p, h, i):
        grp = (2 * p + h) * NT + i
        eng.wait_ge(s_str[i % 2], 64 * (grp // 2 + 1))

    def wstr_upto(eng, p, h):
        grp = (2 * p + h) * NT + NT - 1
        eng.wait_ge(s_str[1], 64 * (grp // 2 + 1))
        eng.wait_ge(s_str[0], 64 * ((grp - 1) // 2 + 1))
    c_swp = sem("c_swp"); c_ev = sem("c_ev"); c_hsw = sem("c_hsw")
    c_hev = sem("c_hev"); c_gt = sem("c_gt"); c_s0 = sem("c_s0")
    c_v0 = sem("c_v0"); c_r0 = sem("c_r0"); c_ap = sem("c_ap")
    c_vA = sem("c_vA"); c_rA = sem("c_rA"); c_pst = sem("c_pst")
    c_v1 = sem("c_v1"); c_r1 = sem("c_r1"); c_xm = sem("c_xm")
    c_vF = sem("c_vF"); c_rF = sem("c_rF"); c_y2 = sem("c_y2")
    c_msr = sem("c_msr"); c_emm = sem("c_emm"); c_mst = sem("c_mst")
    c_slb = sem("c_slb"); c_gtd = sem("c_gtd"); c_gte = sem("c_gte")
    c_vch = sem("c_vch"); c_sch = sem("c_sch"); c_s1q = sem("c_s1q")
    c_tpw = sem("c_tpw"); c_tpe = sem("c_tpe")
    assert KC % 4 == 0
    RPT = KC // 4
    _nv = {"n": 0}
    _ns = {"n": 0}

    def lhsT(k, i):
        base = (k * NT + i) * 128
        return xfm[:, base:base + 128]

    xfm4 = xfm[:].rearrange("p (k i t) -> p k i t", k=KC, i=NT, t=128)

    def strans_ck(b):
        return strans[b][:, 0:D].rearrange("t (ck k) -> t ck k", ck=KC, k=128)

    with nc.Block() as block:

        @block.gpsimd
        def _(g):
            g.dma_start(emt[:], em_ap).then_inc(s_ini, 16)
            g.dma_start(ident[:], id_ap).then_inc(s_ini, 16)
            for p in range(NP):
                KH = KC // 2 * NW
                for w in range(NSLAB if on(p, PH_AMM) else 0):
                    if p == 0 and not c.nomask and c.mask_mode not in ("r2", "r3", "r4", "r5"):
                        for hh in range(2):
                            if w >= 1:
                                g.wait_ge(c_msr, 2 * (w - 1) + hh + 1)
                            g.dma_start(flex_mask[:, hh * KH:(hh + 1) * KH],
                                        mk_ap[w][:, hh * KH:(hh + 1) * KH]
                                        ).then_inc(s_msl[hh], 16)
                    if p > 0 and w < 2:
                        continue
                    if w >= 2:
                        g.wait_ge(c_swp, v_swp(p, 0, w - 2, NT - 1))
                        if p == 0 and not c.nomask and c.mask_mode not in ("r3", "r4", "r5"):
                            wupto(g, s_scw, w - 2)
                    src = wt_ap[w] if p == 0 else wscr[w]
                    iseq(g.dma_start(wsl[w % 2][:], src), s_wsl, e_wsl(p, 0, w))
                if not on(p, PH_BH):
                    continue
                g.wait_ge(c_swp, v_swp(p, 0, NSLAB - 1, NT - 1))
                if p == 0 and not c.nomask and c.mask_mode not in ("r3", "r4", "r5"):
                    wupto(g, s_scw, NSLAB - 1)
                g.dma_start(wsl[1][:, 0:KC * AGW], ag_ap).then_inc(s_ag, 16)
                for w in range(NSLAB if on(p, PH_BMM) else 0):
                    if w == 1:
                        g.wait_ge(c_hsw, v_pi(p, NT - 1))
                    elif w >= 2:
                        g.wait_ge(c_swp, v_swp(p, 1, w - 2, NT - 1))
                    iseq(g.dma_start(wsl[w % 2][:], bw_ap[w]), s_wsl,
                         e_wsl(p, 1, w))
                    iseq(g.dma_start(lbtw[w % 2][:], lb_ap[w]), s_lbt,
                         p * NSLAB + w)
                if p + 1 < NP and on(p + 1, PH_AMM):
                    for w2 in range(2):
                        g.wait_ge(c_swp,
                                  v_swp(p, 1, NSLAB - 2 + w2, NT - 1))
                        iseq(g.dma_start(wsl[w2][:], wscr[w2]), s_wsl,
                             e_wsl(p + 1, 0, w2))
                for i in range(NT if on(p, PH_BPOST) else 0):
                    g.wait_ge(c_pst, v_pi(p, i))
                    if i >= 2:
                        g.wait_ge(c_y2, v_pi(p, i - 2))
                    wseq(g, s_x1w, e_pi(p, i))
                    iseq(g.dma_start(restile[i % 2][:], x1s[i]), s_x1r,
                         e_pi(p, i))
                    if i >= 1:
                        g.wait_ge(c_y2, v_pi(p, i - 1))
                        iseq(g.dma_start(
                            y_ap[p * TP + (i - 1) * 128:p * TP + i * 128, :],
                            x2out[(i - 1) % 2][:]), s_yw, e_pi(p, i - 1))
                if on(p, PH_BPOST):
                    g.wait_ge(c_y2, v_pi(p, NT - 1))
                    iseq(g.dma_start(
                        y_ap[p * TP + (NT - 1) * 128:p * TP + NT * 128, :],
                        x2out[(NT - 1) % 2][:]), s_yw, e_pi(p, NT - 1))

        @block.sync
        def _(sy):
            def tp_xfm(p, i, b):
                n = e_tpx(p, i)
                if n >= 2:
                    wseq_tp(sy, n - 2)
                KH2 = KC // 2
                sy.dma_start(xfm4[:, 0:KH2, i, :],
                             strans_ck(b)[:, 0:KH2, :],
                             transpose=True).then_inc(s_tpx[n % 2], 16)
                sy.dma_start(xfm4[:, KH2:KC, i, :],
                             strans_ck(b)[:, KH2:KC, :],
                             transpose=True).then_inc(s_tpx[n % 2], 16)

            def stg_reads(p, h, i):
                q = NSLAB // 4
                for g_ in range(4):
                    lo = g_ * q
                    if i == 0:
                        wupto(sy, s_stw, e_wsl(p, h, lo + q - 1))
                    src = stg[lo:lo + q, :, i * NW:(i + 1) * NW].transpose(
                        [1, 0, 2])
                    dst = strans[i % 2][:, lo * NW:(lo + q) * NW]
                    sy.dma_start(dst.rearrange("p (a b) -> p a b", a=q, b=NW),
                                 src).then_inc(s_str[i % 2], 16)

            for p in range(NP):
                if not on(p, PH_APRE):
                    continue
                for i in range(NT):
                    if i >= 2:
                        sy.wait_ge(c_ap, v_pi(p, i - 2))
                        sy.wait_ge(c_s0, v_pi(p, i - 2))
                    elif p > 0:
                        gp = v_pi(p - 1, NT - 2 + (i % 2))
                        sy.wait_ge(c_y2, gp)
                    iseq(sy.dma_start(
                        restile[i % 2][:],
                        x_ap[p * TP + i * 128:p * TP + (i + 1) * 128, :]),
                        s_x0a, e_pi(p, i))
                    if i >= 1:
                        sy.wait_ge(c_ap, v_pi(p, i - 1))
                        if p > 0:
                            sy.wait_ge(c_swp, HB(p, 0))
                            sy.wait_ge(c_hsw, v_pi(p - 1, NT - 1))
                        tp_xfm(p, i - 1, (i - 1) % 2)
                sy.wait_ge(c_ap, v_pi(p, NT - 1))
                if p > 0:
                    sy.wait_ge(c_swp, HB(p, 0))
                    sy.wait_ge(c_hsw, v_pi(p - 1, NT - 1))
                tp_xfm(p, NT - 1, (NT - 1) % 2)
                for w in range(NSLAB if on(p, PH_AMM) else 0):
                    if p == 0 and not c.nomask and c.mask_mode not in ("r3", "r4", "r5"):
                        sy.wait_ge(c_slb, w + 1)
                        iseq(sy.dma_start(wscr[w], wsl[w % 2][:]), s_scw, w)
                    if w == 0 and p > 0:
                        wstr_upto(sy, p - 1, 1)
                    sy.wait_ge(c_ev, v_ev(p, 0, w, NT - 1))
                    iseq(sy.dma_start(stg[w][:, 0:NT * NW],
                                      strans[w % 2][:, 0:NT * NW]),
                         s_stw, e_wsl(p, 0, w))
                for i in range(NT if on(p, PH_APOST) else 0):
                    if i >= 2:
                        sy.wait_ge(c_pst, v_pi(p, i - 2))
                        sy.wait_ge(c_xm, v_pi(p, i - 2))
                        if on(p, PH_BH):
                            sy.wait_ge(c_tpe, (p * NT + i - 1) * RPT)
                    if i < 2:
                        sy.wait_ge(c_ap, v_pi(p, NT - 2 + (i % 2)))
                        sy.wait_ge(c_s0, v_pi(p, NT - 2 + (i % 2)))
                    iseq(sy.dma_start(
                        restile[i % 2][:],
                        x_ap[p * TP + i * 128:p * TP + (i + 1) * 128, :]),
                        s_x0r, e_pi(p, i))
                    stg_reads(p, 0, i)
                    sy.wait_ge(c_pst, v_pi(p, i))
                    iseq(sy.dma_start(x1s[i], strans[i % 2][:, 0:D]), s_x1w,
                         e_pi(p, i))
                for i in range(NT if on(p, PH_BH) else 0):
                    sy.wait_ge(c_gt, v_pi(p, i))
                    if e_pi(p, i) >= 2:
                        wseq(sy, s_tph, e_pi(p, i) - 2)
                    iseq(sy.dma_start(
                        hbt.rearrange("p (ck z) -> p ck z", ck=EC, z=TP)[
                            :, :, i * 128:(i + 1) * 128],
                        hfb[:, i * ERP:(i + 1) * ERP].rearrange(
                            "t (ck z) -> t ck z", ck=EC, z=128),
                        transpose=True), s_tph, e_pi(p, i))
                for w in range(NSLAB if on(p, PH_BMM) else 0):
                    if w == 0:
                        wstr_upto(sy, p, 0)
                    sy.wait_ge(c_ev, v_ev(p, 1, w, NT - 1))
                    iseq(sy.dma_start(stg[w][:, 0:NT * NW],
                                      strans[w % 2][:, 0:NT * NW]),
                         s_stw, e_wsl(p, 1, w))
                for i in range(NT if on(p, PH_BPOST) else 0):
                    if i >= 2:
                        sy.wait_ge(c_y2, v_pi(p, i - 2))
                    stg_reads(p, 1, i)
            lp, lph = lim
            if lph >= PH_BPOST:
                wupto(sy, s_yw, lp * NT + NT - 1)
                if "dbg" in io:
                    sy.dma_start(io["dbg"], stats[:]).then_inc(s_ini, 16)
                    sy.wait_ge(s_ini, 48)
            elif lph >= PH_BMM:
                wupto(sy, s_stw, e_wsl(lp, 1, NSLAB - 1))
            elif lph >= PH_BH:
                wupto(sy, s_tph, lp * NT + NT - 1)
            elif lph >= PH_APOST:
                wupto(sy, s_x1w, lp * NT + NT - 1)
            elif lph >= PH_AMM:
                wupto(sy, s_stw, e_wsl(lp, 0, NSLAB - 1))
            else:
                wupto_tp(sy, e_tpx(lp, NT - 1))

        @block.tensor
        def _(t):
            def emm(we):
                B0 = (we % 2) * KC * NB
                for ck in range(KC):
                    ins_ = t.matmul(
                        pbE[:, 256 + ck * NB:256 + (ck + 1) * NB],
                        emt[:], bms[:, B0 + ck * NB:B0 + (ck + 1) * NB],
                        start=True, stop=True)
                return ins_

            _tp0 = {"first": True}
            for p in range(NP):
                if not on(p, PH_AMM):
                    continue
                if p == 0 and not c.nomask and c.mask_mode not in ("red", "r1", "r2", "r3", "r4", "r5"):
                    t.wait_ge(s_ini, 32)
                    t.wait_ge(c_msr, 2)
                    emm(0).then_inc(c_emm, 1)
                for w in range(NSLAB):
                    for i in range(NT):
                        if w == 0:
                            wseq_tp(t, e_tpx(p, i))
                        if p == 0 and not c.nomask and c.mask_mode != "r5":
                            t.wait_ge(c_slb, w + 1)
                        else:
                            wseq(t, s_wsl, e_wsl(p, 0, w))
                        if i >= NBANK:
                            t.wait_ge(c_ev, v_ev(p, 0, w, i - NBANK))
                        elif w >= 1:
                            t.wait_ge(c_ev, v_ev(p, 0, w - 1, bank_last(i)))
                        elif HB(p, 0) > 0:
                            t.wait_ge(c_ev, HB(p, 0))
                        for k in range(KC):
                            ins_ = t.matmul(pb(i)[:, 0:NW], lhsT(k, i),
                                            wsl[w % 2][:, k * NW:(k + 1) * NW],
                                            start=(k == 0), stop=(k == KC - 1))
                        ins_.then_inc(c_swp, 1)
                        if (p == 0 and not c.nomask and i == 0
                                and w + 1 < NSLAB and c.mask_mode not in ("red", "r1", "r2", "r3", "r4", "r5")):
                            we = w + 1
                            t.wait_ge(c_msr, 2 * (we + 1))
                            t.wait_ge(c_mst, we)
                            emm(we).then_inc(c_emm, 1)
                def h_sweep(i):
                    t.wait_ge(s_ag, 16 * (p + 1))
                    t.wait_ge(c_tpe, (p * NT + i + 1) * RPT)
                    if i >= NBANK:
                        t.wait_ge(c_ev, v_ev(p, 1, 0, i - NBANK))
                    else:
                        t.wait_ge(c_ev, v_ev(p, 0, NSLAB - 1, bank_last(i)))
                    for k in range(KC):
                        ins_ = t.matmul(pb(i)[:, 256:256 + AGW], lhsT(k, i),
                                        wsl[1][:, k * AGW:(k + 1) * AGW],
                                        start=(k == 0), stop=(k == KC - 1))
                    ins_.then_inc(c_hsw, 1)

                def base0(i):
                    wseq(t, s_wsl, e_wsl(p, 1, 0))
                    t.wait_ge(c_hev, v_pi(p, i))
                    for k in range(KC):
                        t.matmul(pb(i)[:, 0:NW], lhsT(k, i),
                                 wsl[0][:, k * NW:(k + 1) * NW],
                                 start=(k == 0), stop=False)

                def lora0(i):
                    wseq(t, s_lbt, p * NSLAB + 0)
                    wseq(t, s_tph, e_pi(p, i))
                    for ck in range(EC):
                        ins_ = t.matmul(
                            pb(i)[:, 0:NW],
                            hbt[:, ck * TP + i * 128:ck * TP + (i + 1) * 128],
                            lbtw[0][:, ck * NW:(ck + 1) * NW],
                            start=False, stop=(ck == EC - 1))
                    ins_.then_inc(c_swp, 1)

                def tp_pe(i):
                    gbase = (p * NT + i) * RPT
                    t.wait_ge(c_xm, v_pi(p, i))
                    if _tp0["first"]:
                        t.wait_ge(s_ini, 32)
                        _tp0["first"] = False
                    if i >= NBANK and on(p, PH_BMM):
                        t.wait_ge(c_ev, v_ev(p, 1, 0, i - NBANK))
                    else:
                        t.wait_ge(c_ev, v_ev(p, 0, NSLAB - 1, bank_last(i)))
                    for r in range(RPT):
                        g_ = gbase + r
                        if g_ >= 2:
                            t.wait_ge(c_tpe, g_ - 1)
                        bank = pbM if r % 2 == 0 else pb(i)
                        for cc in range(4):
                            k = r * 4 + cc
                            ins_ = t.matmul(
                                bank[:, cc * 128:(cc + 1) * 128],
                                strans[i % 2][:, k * 128:(k + 1) * 128],
                                ident[:], start=True, stop=True)
                        ins_.then_inc(c_tpw, 1)

                if on(p, PH_BH):
                    for i in range(NT):
                        if i >= NBANK and on(p, PH_BMM):
                            lora0(i - NBANK)
                        tp_pe(i)
                        h_sweep(i)
                        if i >= 1 and on(p, PH_BMM):
                            base0(i - 1)
                    if on(p, PH_BMM):
                        base0(NT - 1)
                        for j in range(max(NT - NBANK, 0), NT):
                            lora0(j)
                for w in (range(1, NSLAB) if on(p, PH_BMM) else []):
                    for i in range(NT):
                        wseq(t, s_wsl, e_wsl(p, 1, w))
                        wseq(t, s_lbt, p * NSLAB + w)
                        if i >= NBANK:
                            t.wait_ge(c_ev, v_ev(p, 1, w, i - NBANK))
                        else:
                            t.wait_ge(c_ev, v_ev(p, 1, w - 1, bank_last(i)))
                        for k in range(KC):
                            t.matmul(pb(i)[:, 0:NW], lhsT(k, i),
                                     wsl[w % 2][:, k * NW:(k + 1) * NW],
                                     start=(k == 0), stop=False)
                        for ck in range(EC):
                            ins_ = t.matmul(
                                pb(i)[:, 0:NW],
                                hbt[:, ck * TP + i * 128:ck * TP + (i + 1) * 128],
                                lbtw[w % 2][:, ck * NW:(ck + 1) * NW],
                                start=False, stop=(ck == EC - 1))
                        ins_.then_inc(c_swp, 1)

        @block.scalar
        def _(sc):
            _ne = {"n": 0}

            def rstd_chain(i, dst, done_sem):
                sc.activation(scol("tln", i), scol("tvar", i), AF.Ln,
                              bias=consts[:, 0:1]).then_inc(c_sch, 1)
                _ns["n"] += 1
                sc.wait_ge(c_sch, _ns["n"])
                sc.activation(scol(dst, i), scol("tln", i), AF.Exp,
                              scale=consts[:, 1:2]).then_inc(done_sem, 1)

            for p in range(NP):
                for i in range(NT if on(p, PH_APRE) else 0):
                    wseq(sc, s_x0a, e_pi(p, i))
                    if i >= 2:
                        wseq_tp(sc, e_tpx(p, i - 2))
                    sc.activation(strans[i % 2][:, 0:D], restile[i % 2][:], AF.Square,
                                  accum_out=scol("sq0", i)).then_inc(c_s0, 1)
                    sc.wait_ge(c_v0, v_pi(p, i))
                    rstd_chain(i, "r0", c_r0)
                for w in range(NSLAB if on(p, PH_AMM) else 0):
                    for i in range(NT):
                        sc.wait_ge(c_swp, v_swp(p, 0, w, i))
                        if w == 0 and i == 0:
                            wupto_tp(sc, e_tpx(p, NT - 1))
                        if w >= 2 and i == 0:
                            wseq(sc, s_stw, e_wsl(p, 0, w - 2))
                        sc.activation(strans[w % 2][:, i * NW:(i + 1) * NW],
                                      pb(i)[:, 0:NW], AF.Copy,
                                      accum_out=scol("acol", i * NSLAB + w))
                        sc.wait_ge(c_ev, _ne["n"])
                        sc.activation(dump[:], pb(i)[:, 0:NW], AF.Square,
                                      accum_out=scol("qcol", i * NSLAB + w)
                                      ).then_inc(c_ev, 1)
                        _ne["n"] += 1
                def evacB(w, i):
                    sc.wait_ge(c_swp, v_swp(p, 1, w, i))
                    if w == 0 and i == 0:
                        sc.wait_ge(c_tpe, (p * NT + NT - 1) * RPT)
                    if w == 1 and i == 0:
                        sc.wait_ge(c_tpe, (p * NT + NT) * RPT)
                    if w >= 2 and i == 0:
                        wseq(sc, s_stw, e_wsl(p, 1, w - 2))
                    sc.activation(strans[w % 2][:, i * NW:(i + 1) * NW],
                                  pb(i)[:, 0:NW], AF.Copy,
                                  accum_out=scol("acol", i * NSLAB + w))
                    sc.wait_ge(c_ev, _ne["n"])
                    sc.activation(dump[:], pb(i)[:, 0:NW], AF.Square,
                                  accum_out=scol("qcol", i * NSLAB + w)
                                  ).then_inc(c_ev, 1)
                    _ne["n"] += 1

                def tp_evac(i):
                    gbase = (p * NT + i) * RPT
                    for r in range(RPT):
                        sc.wait_ge(c_tpw, gbase + r + 1)
                        bank = pbM if r % 2 == 0 else pb(i)
                        sc.activation(xfm4[:, 4 * r:4 * r + 4, i, :],
                                      bank[:].rearrange("p (a b) -> p a b",
                                                        a=4, b=128),
                                      AF.Copy).then_inc(c_tpe, 1)

                def hev_gate(j):
                    sc.wait_ge(c_hsw, v_pi(p, j))
                    sc.activation(hraw[:, j * AGW:(j + 1) * AGW],
                                  pb(j)[:, 256:256 + AGW], AF.Copy
                                  ).then_inc(c_hev, 1)
                    sc.wait_ge(c_gtd, v_pi(p, j))
                    sc.activation(scol("ecol"), scol("dcol"), AF.Exp
                                  ).then_inc(c_gte, 1)

                for i in range(NT if on(p, PH_APOST) else 0):
                    if i == 0:
                        sc.wait_ge(c_vA, v_pi(p, 0))
                        rstd_chain(0, "rA", c_rA)
                    sc.wait_ge(c_pst, v_pi(p, i))
                    sc.activation(x2out[i % 2][:], strans[i % 2][:, 0:D],
                                  AF.Square,
                                  accum_out=scol("x1sq", i)).then_inc(c_s1q, 1)
                    if i >= 1 and on(p, PH_BH):
                        if i - 1 >= NBANK and on(p, PH_BMM):
                            evacB(0, i - 1 - NBANK)
                        hev_gate(i - 1)
                    sc.wait_ge(c_v1, v_pi(p, i))
                    rstd_chain(i, "r1", c_r1)
                    if i + 1 < NT:
                        sc.wait_ge(c_vA, v_pi(p, i + 1))
                        rstd_chain(i + 1, "rA", c_rA)
                    if on(p, PH_BH) and (i < NT - 1 or NT - 1 < NBANK):
                        tp_evac(i)
                if on(p, PH_BH):
                    if NT - 1 >= NBANK:
                        if on(p, PH_BMM):
                            evacB(0, NT - 1 - NBANK)
                        tp_evac(NT - 1)
                    hev_gate(NT - 1)
                if on(p, PH_BH) and on(p, PH_BMM):
                    for j in range(max(NT - NBANK, 0), NT):
                        evacB(0, j)
                for w in (range(1, NSLAB) if on(p, PH_BMM) else []):
                    for i in range(NT):
                        evacB(w, i)
                for i in range(NT if on(p, PH_BPOST) else 0):
                    sc.wait_ge(c_vF, v_pi(p, i))
                    rstd_chain(i, "rA", c_rF)

        @block.vector
        def _(v):
            def vf(ins):
                ins.then_inc(c_vch, 1)
                _nv["n"] += 1

            def vw():
                v.wait_ge(c_vch, _nv["n"])

            def var_chain(sumc, sqc, muc, i):
                vw()
                vf(v.tensor_scalar(scol(muc, i), scol(sumc, i), rD, None,
                                   ALU.mult))
                vw()
                vf(v.tensor_tensor(scol("tm2", i), scol(muc, i), scol(muc, i),
                                   ALU.mult))
                vw()
                return v.scalar_tensor_tensor(scol("tvar", i), scol(sqc, i), rD,
                                              scol("tm2", i), ALU.mult,
                                              ALU.subtract)

            v.memset(consts[:, 0:1], float(c.eps))
            v.memset(consts[:, 1:2], -0.5)
            for p in range(NP):
                state = {"mask_w": 0, "prep_w": 0}

                def emit_mask_prep():
                    w = state["prep_w"]
                    B0 = (w % 2) * KC * NB
                    if c.mask_mode in ("r1", "r2", "r3", "r4", "r5"):
                        v.memset(bms[:, B0:B0 + KC * NB], 0.0).then_inc(c_msr, 1)
                        v.memset(bms[:, B0:B0 + KC * NB], 0.0).then_inc(c_msr, 1)
                    else:
                        KH2 = KC // 2 * NW
                        BH2 = KC // 2 * NB
                        if w >= 2:
                            v.wait_ge(c_emm, w - 1)
                        for hh in range(2):
                            v.wait_ge(s_msl[hh], 16 * (w + 1))
                            with nc.allow_low_precision(
                                    reason="mask 0/1 block sums <=32 exact"):
                                v.tensor_reduce(
                                    bms[:, B0 + hh * BH2:B0 + (hh + 1) * BH2],
                                    flex_mask[:, hh * KH2:(hh + 1) * KH2
                                              ].rearrange(
                                        "p (ck b z) -> p ck b z", ck=KC // 2,
                                        b=NB, z=32),
                                    AX.X, ALU.add).then_inc(c_msr, 1)
                    state["prep_w"] += 1

                def emit_mask_mult():
                    w = state["mask_w"]
                    if c.mask_mode not in ("red", "r1", "r2", "r3", "r4", "r5"):
                        v.wait_ge(c_emm, w + 1)
                        v.tensor_scalar(pexp_all[:, w * KC * NB:(w + 1) * KC * NB],
                                        pbE[:, 256:256 + KC * NB], 1.0,
                                        None, ALU.min).then_inc(c_mst, 1)
                    wseq(v, s_wsl, e_wsl(p, 0, w))
                    if c.mask_mode == "full":
                        v.wait_ge(c_mst, w + 1)
                        v4 = wsl[w % 2][:].rearrange("p (ck b z) -> p ck b z",
                                                     ck=KC, b=NB, z=32)
                        v.tensor_tensor(
                            v4, v4,
                            pexp_all[:, w * KC * NB:(w + 1) * KC * NB
                                     ].rearrange("p (ck b) -> p ck b", ck=KC,
                                                 b=NB).unsqueeze(3
                                                 ).to_broadcast(
                                                     [128, KC, NB, 32]),
                            ALU.mult).then_inc(c_slb, 1)
                    elif c.mask_mode in ("r4", "r5"):
                        v.memset(bms[:], 1.0).then_inc(c_slb, 1)
                    else:
                        v.tensor_scalar(wsl[w % 2][:], wsl[w % 2][:], 1.0,
                                        None, ALU.mult).then_inc(c_slb, 1)
                    state["mask_w"] += 1

                for i in range(NT if on(p, PH_APRE) else 0):
                    wseq(v, s_x0a, e_pi(p, i))
                    vf(v.tensor_reduce(scol("sum0", i), restile[i % 2][:], AX.X,
                                       ALU.add))
                    v.wait_ge(c_s0, v_pi(p, i))
                    var_chain("sum0", "sq0", "mu0", i).then_inc(c_v0, 1)
                    v.wait_ge(c_r0, v_pi(p, i))
                    v.tensor_scalar(strans[i % 2][:, 0:D], restile[i % 2][:],
                                    scol("mu0", i), scol("r0", i),
                                    ALU.subtract, ALU.mult).then_inc(c_ap, 1)
                    if p == 0 and on(p, PH_AMM) and not c.nomask:
                        while state["prep_w"] < min(i + 2, 3, NSLAB):
                            emit_mask_prep()
                        while state["mask_w"] < min(i + 1, 2):
                            emit_mask_mult()
                if p == 0 and on(p, PH_AMM) and not c.nomask:
                    while state["mask_w"] < NSLAB or state["prep_w"] < NSLAB:
                        if state["prep_w"] < min(state["mask_w"] + 1, NSLAB):
                            emit_mask_prep()
                        else:
                            emit_mask_mult()
                def _gating_body(i):
                    lg = hraw[:, i * AGW + ER:(i + 1) * AGW]
                    vw()
                    vf(v.max(out=stats[:, off["m8"]:off["m8"] + 8], in_=lg))
                    vw()
                    vf(v.memset(stats[:, off["m8"] + 2:off["m8"] + 8], NEG_BIG))
                    vw()
                    vf(v.match_replace(out=scol("zap2", 0, NE),
                                       in_to_replace=stats[:, off["m8"]:off["m8"] + 8],
                                       in_values=lg, imm_value=NEG_BIG))
                    vw()
                    vf(v.tensor_copy(stats[:, off["m8b"]:off["m8b"] + 8],
                                     stats[:, off["m8"]:off["m8"] + 8]))
                    vw()
                    vf(v.memset(stats[:, off["m8b"] + 1:off["m8b"] + 8],
                                NEG_BIG))
                    vw()
                    vf(v.match_replace(out=scol("zap1", 0, NE),
                                       in_to_replace=stats[:, off["m8b"]:off["m8b"] + 8],
                                       in_values=lg, imm_value=NEG_BIG))
                    vw()
                    vf(v.tensor_tensor(scol("ind1", 0, NE), scol("zap1", 0, NE),
                                       lg, ALU.not_equal))
                    vf(v.tensor_tensor(scol("ind12", 0, NE),
                                       scol("zap2", 0, NE), lg, ALU.not_equal))
                    v.tensor_tensor(scol("dcol"),
                                    stats[:, off["m8"] + 1:off["m8"] + 2],
                                    stats[:, off["m8"]:off["m8"] + 1],
                                    ALU.subtract).then_inc(c_gtd, 1)
                    v.wait_ge(c_gte, v_pi(p, i))
                    vf(v.tensor_scalar(scol("opl"), scol("ecol"), 1.0, None,
                                       ALU.add))
                    vw()
                    vf(v.reciprocal(scol("p1"), scol("opl")))
                    vw()
                    vf(v.tensor_tensor(scol("p2"), scol("ecol"), scol("p1"),
                                       ALU.mult))
                    vw()
                    vf(v.tensor_tensor(scol("t2"), scol("p1"), scol("p2"),
                                       ALU.subtract))
                    vw()
                    vf(v.tensor_scalar(scol("c1"), scol("t2"), c.SCALING, None,
                                       ALU.mult))
                    vf(v.tensor_scalar(scol("c2"), scol("p2"), c.SCALING, None,
                                       ALU.mult))
                    vw()
                    vf(v.tensor_scalar(scol("t3", 0, NE), scol("ind12", 0, NE),
                                       scol("c2"), None, ALU.mult))
                    vw()
                    vf(v.scalar_tensor_tensor(scol("ind1", 0, NE),
                                              scol("ind1", 0, NE), scol("c1"),
                                              scol("t3", 0, NE), ALU.mult,
                                              ALU.add))
                    vw()
                    v.tensor_tensor(
                        hfb[:, i * ERP:i * ERP + ER].rearrange(
                            "p (e r) -> p e r", e=NE, r=c.R),
                        hraw[:, i * AGW:i * AGW + ER].rearrange(
                            "p (e r) -> p e r", e=NE, r=c.R),
                        scol("ind1", 0, NE).unsqueeze(2).to_broadcast(
                            [128, NE, c.R]),
                        ALU.mult).then_inc(c_gt, 1)

                def emit_gating(i):
                    if not on(p, PH_BH):
                        return
                    v.wait_ge(c_hev, v_pi(p, i))
                    _gating_body(i)

                if on(p, PH_BH) and on(p, PH_APOST):
                    vf(v.memset(hfb[:], 0.0))
                def reds_varA(i):
                    v.wait_ge(c_ev, v_ev(p, 0, NSLAB - 1, NT - 1))
                    v.tensor_reduce(scol("asum", i),
                                    scol("acol", i * NSLAB, NSLAB), AX.X,
                                    ALU.add)
                    vf(v.tensor_reduce(scol("qsum", i),
                                       scol("qcol", i * NSLAB, NSLAB), AX.X,
                                       ALU.add))
                    var_chain("asum", "qsum", "muA", i).then_inc(c_vA, 1)

                for i in range(NT if on(p, PH_APOST) else 0):
                    if i == 0:
                        reds_varA(0)
                    v.wait_ge(c_rA, v_pi(p, i))
                    wstr_grp(v, p, 0, i)
                    wseq(v, s_x0r, e_pi(p, i))
                    if p > 0 and i < 2:
                        wseq(v, s_yw, e_pi(p - 1, NT - 2 + (i % 2)))
                    if i >= 2:
                        v.wait_ge(c_s1q, v_pi(p, i - 2))
                    vf(v.tensor_scalar(x2out[i % 2][:], strans[i % 2][:, 0:D],
                                       scol("muA", i), scol("rA", i),
                                       ALU.subtract, ALU.mult))
                    vw()
                    v.scalar_tensor_tensor(strans[i % 2][:, 0:D],
                                           x2out[i % 2][:],
                                           1.0, restile[i % 2][:], ALU.mult,
                                           ALU.add,
                                           accum_out=scol("x1sum", i)
                                           ).then_inc(c_pst, 1)
                    v.wait_ge(c_pst, v_pi(p, i))
                    if i >= 1:
                        emit_gating(i - 1)
                    v.wait_ge(c_s1q, v_pi(p, i))
                    var_chain("x1sum", "x1sq", "mu1", i).then_inc(c_v1, 1)
                    v.wait_ge(c_r1, v_pi(p, i))
                    wseq(v, s_x1w, e_pi(p, i))
                    v.tensor_scalar(strans[i % 2][:, 0:D],
                                    strans[i % 2][:, 0:D],
                                    scol("mu1", i), scol("r1", i),
                                    ALU.subtract, ALU.mult).then_inc(c_xm, 1)
                    if i + 1 < NT:
                        reds_varA(i + 1)
                if on(p, PH_BH):
                    emit_gating(NT - 1)
                for i in range(NT if on(p, PH_BPOST) else 0):
                    v.wait_ge(c_ev, v_ev(p, 1, NSLAB - 1, NT - 1))
                    v.tensor_reduce(scol("asum", i),
                                    scol("acol", i * NSLAB, NSLAB), AX.X,
                                    ALU.add)
                    vf(v.tensor_reduce(scol("qsum", i),
                                       scol("qcol", i * NSLAB, NSLAB), AX.X,
                                       ALU.add))
                    var_chain("asum", "qsum", "muA", i).then_inc(c_vF, 1)
                    v.wait_ge(c_rF, v_pi(p, i))
                    wstr_grp(v, p, 1, i)
                    wseq(v, s_x1r, e_pi(p, i))
                    if i >= 2:
                        wseq(v, s_yw, e_pi(p, i - 2))
                    vf(v.tensor_scalar(x2out[i % 2][:], strans[i % 2][:, 0:D],
                                       scol("muA", i), scol("rA", i),
                                       ALU.subtract, ALU.mult))
                    vw()
                    v.scalar_tensor_tensor(x2out[i % 2][:], x2out[i % 2][:], 1.0,
                                           restile[i % 2][:], ALU.mult, ALU.add
                                           ).then_inc(c_y2, 1)

    st_.close()
    return nc


def _run(inputs, c: Cfg, n_cores=8, lim=None):
    T = c.NPASS * c.TP
    x = np.ascontiguousarray(
        np.asarray(inputs["L0_attn_data"], dtype=np.float32).reshape(-1, c.D))
    host = host_layouts(
        c,
        np.asarray(inputs["L0_attn_weight"], dtype=np.float32),
        np.asarray(inputs["L0_attn_mask"], dtype=np.int32),
        np.asarray(inputs["L0_ffn_base_W"], dtype=np.float32),
        np.asarray(inputs["L0_ffn_gate_W"], dtype=np.float32),
        np.asarray(inputs["L0_ffn_lora_A"], dtype=np.float32),
        np.asarray(inputs["L0_ffn_lora_B"], dtype=np.float32),
    )
    nc = bass.Bass("TRN2", target_bir_lowering=False)
    io = {}
    io["x"] = nc.dram_tensor("x", [T, c.D], F32, kind="ExternalInput").ap()
    io["y"] = nc.dram_tensor("y", [T, c.D], F32, kind="ExternalOutput").ap()
    for nm, dt_ in [("wt", F32), ("mk", I32), ("bw", F32), ("ag", F32),
                    ("lb", F32)]:
        io[nm] = nc.dram_tensor(nm, list(host[nm].shape), dt_,
                                kind="ExternalInput").ap()
    io["em"] = nc.dram_tensor("em", [128, 128], BF16, kind="ExternalInput").ap()
    io["id"] = nc.dram_tensor("id", [128, 128], BF16, kind="ExternalInput").ap()
    if getattr(c, "debug_stats", False):
        io["dbg"] = nc.dram_tensor("dbg", [128, 1024], F32,
                                   kind="ExternalOutput").ap()
    build(nc, c, io, lim=lim)
    in_maps = []
    for ci in range(n_cores):
        m = dict(host)
        m["x"] = np.ascontiguousarray(x[ci * T:(ci + 1) * T])
        in_maps.append(m)
    res = run_bass_kernel_spmd(nc, in_maps, core_ids=list(range(n_cores)))
    out = np.concatenate([r["y"] for r in res.results], axis=0)
    if getattr(c, "debug_stats", False):
        return out.reshape(1, -1, c.D), [r["dbg"] for r in res.results]
    return out.reshape(1, -1, c.D)


def kernel(L0_attn_data, L0_attn_weight, L0_attn_mask, L0_ffn_base_W,
           L0_ffn_gate_W, L0_ffn_lora_A, L0_ffn_lora_B):
    c = Cfg()
    return _run(dict(
        L0_attn_data=L0_attn_data, L0_attn_weight=L0_attn_weight,
        L0_attn_mask=L0_attn_mask, L0_ffn_base_W=L0_ffn_base_W,
        L0_ffn_gate_W=L0_ffn_gate_W, L0_ffn_lora_A=L0_ffn_lora_A,
        L0_ffn_lora_B=L0_ffn_lora_B), c)
